# version 32 (final)
# baseline (speedup 1.0000x reference)
import numpy as np
import ml_dtypes
import concourse.bass as bass
import concourse.mybir as mybir
from concourse.bass_utils import run_bass_kernel_spmd

F32 = mybir.dt.float32
BF16 = mybir.dt.bfloat16
AF = mybir.ActivationFunctionType
ALU = mybir.AluOpType
f32 = np.float32
bfl = ml_dtypes.bfloat16

D = 1024
T = 2048
CH = 8
NP = 8
DH = 64
EPS = f32(1e-5)
SHIFT = f32(-32.0)
SCALE8 = f32(0.125)

class Sem:
    def __init__(self, h):
        self.h = h
        self.n = 0

class Em:
    ENGS = ("sync", "tensor", "vector", "scalar", "gpsimd")
    SERIAL = {"vector": "dve", "scalar": "act", "gpsimd": "pool"}

    def __init__(self, S=None):
        self.streams = {e: [] for e in self.ENGS}
        self.S = S

    def do(self, eng, f, inc=None, inc_n=1):
        if eng in self.SERIAL:
            s = self.S[self.SERIAL[eng]]
            assert inc is None or inc is s, "engine ops may only inc their own sem"
            v = s.n
            if v > 0:
                self.streams[eng].append(lambda e, s=s, v=v: e.wait_ge(s.h, v))
            s.n += 1
            self.streams[eng].append(lambda e, f=f, s=s: f(e).then_inc(s.h, 1))
            return
        if eng == "tensor":
            s = self.S["pe"]
            assert inc is None or inc is s
            s.n += 1
            self.streams[eng].append(lambda e, f=f, s=s: f(e).then_inc(s.h, 1))
            return
        if inc is not None:
            inc.n += inc_n
            n, s = inc_n, inc
            self.streams[eng].append(lambda e, f=f: f(e).then_inc(s.h, n))
        else:
            self.streams[eng].append(lambda e, f=f: f(e))

    def wait(self, eng, sem, val=None):
        v = sem.n if val is None else val
        assert v <= sem.n, "waiting for a future count"
        if v > 0:
            self.streams[eng].append(lambda e, s=sem, v=v: e.wait_ge(s.h, v))

    def replay(self, block):
        for eng in self.ENGS:
            ops = self.streams[eng]
            if not ops:
                continue
            def body(e, ops=ops):
                for op in ops:
                    op(e)
            getattr(block, eng)(body)

class MK:
    def __init__(self, blocks, debug=()):
        self.blocks = blocks
        self.debug = set(debug)
        nc = bass.Bass("TRN2", target_bir_lowering=False, dynamic_dma_scratch_size=8192)
        self.nc = nc
        self.dbg_outs = {}

        self.x_in = nc.dram_tensor("x_in", [D, T], F32, kind="ExternalInput").ap()
        self.x_out = nc.dram_tensor("x_out", [D, T], F32, kind="ExternalOutput").ap()
        self.x_spill = nc.dram_tensor("x_spill", [D, T], BF16, kind="Internal").ap()
        self.ins = {}
        def di(name, shape, dtype=F32):
            self.ins[name] = nc.dram_tensor(name, shape, dtype, kind="ExternalInput").ap()
        if "l0a" in blocks:
            di("wq0", [D, D]); di("wkv0", [D, D]); di("wout0", [D, D])
            di("bout0", [128, CH])
            di("cosf", [128, T], BF16); di("sinf", [128, T], BF16)
            di("permT", [128, 128], BF16)
        if "l0f" in blocks:
            di("w0T", [D, D]); di("rT", [D, 8]); di("aT", [D, 128]); di("bT", [128, D])
            di("b0", [128, CH])
        if "l1a" in blocks:
            di("wq1", [D, D]); di("wk1", [D, D]); di("wv1", [D, D]); di("wo1", [D, D])
            di("bq1", [128, CH]); di("bk1", [128, CH]); di("bo1", [128, CH])
            di("bv1row", [1, D])
        if "l1f" in blocks:
            di("lnsc", [128, CH]); di("lnbi", [128, CH])
            di("awT", [D, D]); di("ab1", [128, CH])
            di("w1a", [D, 4096]); di("w1b", [D, 4096])
            di("w2a", [4096, D]); di("w2b", [4096, D])
            di("b1a", [128, 32]); di("b1b", [128, 32])
            di("b2a", [128, CH]); di("b2b", [128, CH])
            di("nwT", [D, D]); di("nb", [128, CH])

        sb = nc.alloc_sbuf_tensor
        self.xb = sb("xb", [128, CH * T], BF16)
        self.xn = sb("xn", [128, CH * T], BF16)
        self.outT = sb("outTb", [128, CH * T], BF16)
        self.pairs = sb("pairs", [128, 8192], BF16)
        self.qtp = self.pairs.ap()[:, 0:T]
        self.kvp = self.pairs.ap()[:, T:2 * T]
        self.pt = [self.pairs.ap()[:, 2 * T + i * T: 2 * T + (i + 1) * T] for i in range(2)]
        self.ubuf = self.pairs
        self.vp = sb("vp", [128, 2 * T], BF16, is_transpose_dest=True)
        self.warena = sb("warena", [128, 8192], BF16)
        self.stage = [sb(f"stage{i}", [128, 1024], F32) for i in range(2)]
        self.scr16 = sb("scr16", [128, 6144], BF16)
        self.lncast = self.scr16.ap()[:, 0:T]
        self.lnsq = self.scr16.ap()[:, T:2 * T]
        self.aptmp = [self.scr16.ap()[:, (1 + i) * T:(2 + i) * T] for i in range(2)]
        self.stage2nd = [self.scr16.ap()[:, 0:T].bitcast(F32),
                         self.scr16.ap()[:, T:2 * T].bitcast(F32)]
        self.mB = sb("mB", [128, T], BF16)
        self.rB = sb("rB", [128, T], BF16)
        self.rowb = sb("rowb", [128, T], BF16)
        self.rowb2 = sb("rowb2", [128, 1024], BF16)
        self.rotc = sb("rotc", [128, 2 * T + 128], BF16) if "l0a" in blocks else None
        self.ident = sb("ident", [128, 128], BF16)
        self.tril = sb("tril", [128, 128], BF16)
        self.onesD = sb("onesD", [128, 1], BF16)
        self.ones1r = sb("ones1r", [1, 128], BF16)
        self.onespad = sb("onespad", [128, 32], BF16)
        self.selER = sb("selER", [8, 128], BF16)
        self.e33 = sb("e33", [33, 128], BF16)
        self.bias = {}
        for name in ("bout0", "b0", "bq1", "bk1", "bo1", "lnsc", "lnbi", "ab1",
                     "b1a", "b1b", "b2a", "b2b", "nb"):
            if name in self.ins:
                self.bias[name] = sb("sb_" + name, list(self.ins[name].shape), F32)
        if "l1a" in blocks:
            self.bvrow = sb("sb_bvrow", [1, D], BF16)
        if "l0f" in blocks:
            self.gl_tok = None
            self.gscr = None
            self.gscr2 = None
            self.gsml = None
            self.gate_bf = sb("gate_bf", [128, 128], BF16)

        self.psA = nc.alloc_psum_tensor("psA", [128, 4096], F32)

        import contextlib
        self._es = contextlib.ExitStack()
        self.S = {}
        for name in ("pe", "act", "dve", "pool", "sy", "sy2", "syd"):
            self.S[name] = Sem(self._es.enter_context(nc.semaphore(name)))
        self.em = Em(self.S)

        for name in self.debug:
            shp = {"xn": [D, T], "z": [D, T], "outT": [D, T],
                   "qt": [128, T], "kv": [128, T], "vp": [128, T],
                   "pt": [128, T], "mrow": [1, T], "rrow": [1, T],
                   "u": [D, T], "gate": [128, 128], "h0": [128, T],
                   "gh": [128, T], "y": [D, T], "t": [D, T], "xa": [D, T],
                   "eo": [D, T]}[name]
            self.dbg_outs[name] = nc.dram_tensor("dbg_" + name, shp, F32,
                                                 kind="ExternalOutput").ap()

    def eo_chunk(self, c):
        if c < 6:
            return self.scr16.ap()[:, c * 1024:(c + 1) * 1024]
        return (self.mB if c == 6 else self.rB).ap()[:, 0:1024]

    def chunks(self, buf, cols=T):
        return lambda c: buf.ap()[:, c * cols:(c + 1) * cols]

    def wait_w(self, eng):
        self.em.wait(eng, self.S["pool"])
        self.em.wait(eng, self.S["dve"])

    def dsig(self):
        self.em.wait("sync", self.S["sy"])
        self.em.do("sync", lambda e: e.nop(), inc=self.S["syd"])
        return self.S["syd"]

    def dump_full(self, name, buf, n_chunks=CH, cols=T):
        if name not in self.dbg_outs:
            return
        em, S = self.em, self.S
        out = self.dbg_outs[name]
        for c in range(n_chunks):
            w = min(1024, cols)
            for j0 in range(0, cols, w):
                em.do("scalar", lambda e, c=c, j0=j0, w=w: e.copy(
                    out=self.stage[0][:, 0:w],
                    in_=buf.ap()[:, c * cols + j0: c * cols + j0 + w]), inc=S["act"])
                em.wait("sync", S["act"])
                em.do("sync", lambda e, c=c, j0=j0, w=w: e.dma_start(
                    out=out[c * 128:(c + 1) * 128, j0:j0 + w],
                    in_=self.stage[0][:, 0:w]), inc=S["sy"], inc_n=16)
                self.dsig()
                em.wait("scalar", S["syd"])

    def dump_one(self, name, ap, rows=128):
        if name not in self.dbg_outs:
            return
        em, S = self.em, self.S
        out = self.dbg_outs[name]
        cols = ap.shape[-1]
        em.wait("scalar", S["dve"])
        em.wait("scalar", S["pool"])
        for j0 in range(0, cols, 1024):
            w = min(1024, cols - j0)
            em.do("scalar", lambda e, j0=j0, w=w: e.copy(
                out=self.stage[0][0:rows, 0:w], in_=ap[:, j0:j0 + w]), inc=S["act"])
            em.wait("sync", S["act"])
            em.do("sync", lambda e, j0=j0, w=w: e.dma_start(
                out=out[0:rows, j0:j0 + w], in_=self.stage[0][0:rows, 0:w]),
                inc=S["sy"], inc_n=16)
            self.dsig()
            em.wait("scalar", S["syd"])

    def emit_consts(self):
        em, S = self.em, self.S
        g = "gpsimd"
        def pdo(f):
            em.do(g, f)
        pdo(lambda e: e.memset(self.ident[:], 1.0))
        pdo(lambda e: e.affine_select(out=self.ident[:], in_=self.ident[:],
              pattern=[[1, 128]], compare_op=ALU.is_equal, fill=0.0, base=0, channel_multiplier=-1))
        pdo(lambda e: e.memset(self.tril[:], 1.0))
        pdo(lambda e: e.affine_select(out=self.tril[:], in_=self.tril[:],
              pattern=[[1, 128]], compare_op=ALU.is_ge, fill=0.0, base=0, channel_multiplier=-1))
        pdo(lambda e: e.memset(self.onesD[:], 1.0 / 1024.0))
        pdo(lambda e: e.memset(self.ones1r[:], 1.0))
        pdo(lambda e: e.memset(self.onespad[:], 1.0))
        pdo(lambda e: e.memset(self.selER[:], 1.0))
        pdo(lambda e: e.affine_select(out=self.selER[:], in_=self.selER[:],
              pattern=[[1, 128]], compare_op=ALU.is_ge, fill=0.0, base=0, channel_multiplier=-16))
        pdo(lambda e: e.affine_select(out=self.selER[:], in_=self.selER[:],
              pattern=[[-1, 128]], compare_op=ALU.is_ge, fill=0.0, base=15, channel_multiplier=16))
        pdo(lambda e: e.memset(self.e33[:], 0.0))
        pdo(lambda e: e.memset(self.e33[0:1, 0:64], 1.0))
        pdo(lambda e: e.memset(self.e33[32:33, 64:128], 1.0))
        pdo(lambda e: e.memset(self.rowb2[:], 0.0))
        xact_marks = {}
        prev_dsem = None
        for c in range(CH):
            dsem = S["sy"] if c % 2 == 0 else S["sy2"]
            sth = self.stage[c % 2]
            if c >= 2:
                em.wait("sync", S["act"], xact_marks[c - 2])
            em.do("sync", lambda e, c=c, sth=sth: e.dma_start(
                out=sth[:], in_=self.x_in[c * 128:(c + 1) * 128, 0:1024]), inc=dsem, inc_n=16)
            em.do("sync", lambda e, c=c, sth=sth: e.dma_start(
                out=self.stage2nd[c % 2][:], in_=self.x_in[c * 128:(c + 1) * 128, 1024:2048]),
                inc=dsem, inc_n=16)
            em.wait("sync", dsem)
            em.do("sync", lambda e: e.nop(), inc=S["syd"])
            em.wait("scalar", S["syd"])
            em.do("scalar", lambda e, c=c, sth=sth: e.copy(
                out=self.xb.ap()[:, c * T: c * T + 1024], in_=sth[:]))
            em.do("scalar", lambda e, c=c: e.copy(
                out=self.xb.ap()[:, c * T + 1024: c * T + 2048], in_=self.stage2nd[c % 2][:]))
            xact_marks[c] = S["act"].n
        for name, sbuf in self.bias.items():
            em.do("sync", lambda e, n=name, s=sbuf: e.dma_start(out=s[:], in_=self.ins[n]),
                  inc=S["sy"], inc_n=16)
        if "l1a" in self.blocks:
            em.wait("sync", S["act"])
            em.do("sync", lambda e: e.dma_start(out=self.stage[0][0:1, :], in_=self.ins["bv1row"]),
                  inc=S["sy"], inc_n=16)
        self.dsig()
        if "l1a" in self.blocks:
            em.wait("vector", S["syd"])
            em.do("vector", lambda e: e.tensor_copy(out=self.bvrow[:], in_=self.stage[0][0:1, :]),
                  inc=S["dve"])
        self._stage_gate = (S["act"].n, S["dve"].n, S["pool"].n)
        em.wait("vector", S["syd"])
        em.wait("vector", S["pool"])
        em.wait("tensor", S["pool"])
        em.wait("gpsimd", S["syd"])
        em.wait("scalar", S["syd"])

    def emit_ln(self, srcbf=None, use_x32_cast=False, dst=None, mode="norm",
               gamma=None, beta=None, half=None, src_marks=None, sq_eng=None,
               sq_scratch=None):
        em, S = self.em, self.S
        em.wait("gpsimd", S["act"]); em.wait("gpsimd", S["dve"])
        em.wait("vector", S["act"]); em.wait("vector", S["pool"])
        if sq_eng != "scalar":
            em.wait("scalar", S["dve"])
        em.wait("scalar", S["pool"])
        g = getattr(self, "_psA_guard", None)
        if g is None:
            em.wait("tensor", S["act"]); em.wait("tensor", S["dve"])
        else:
            em.wait("tensor", S["act"], g[0]); em.wait("tensor", S["dve"], g[1])
        self._psA_guard = None
        st1 = self.psA.ap()[0:1, 0:2048]
        st2 = self.psA.ap()[32:33, 2048:4096]
        sqb = [self.lnsq, self.mB.ap()[:, 0:T]]
        pe_marks = []
        for c in range(CH):
            if use_x32_cast:
                src_c = self.xb.ap()[:, c * T:(c + 1) * T]
            else:
                src_c = srcbf(c)
            if sq_eng == "scalar":
                sq = sq_scratch(c)
                if src_marks is not None and c in src_marks:
                    em.wait("scalar", S["dve"], src_marks[c])
                em.do("scalar", lambda e, s=src_c, sq=sq: e.activation(
                    out=sq, in_=s, func=AF.Square))
                sqm = S["act"].n
                if src_marks is not None and c in src_marks:
                    em.wait("tensor", S["dve"], src_marks[c])
                em.wait("tensor", S["act"], sqm)
            else:
                sq = sqb[c % 2]
                if c >= 2:
                    em.wait("vector", S["pe"], pe_marks[c - 2])
                em.do("vector", lambda e, s=src_c, sq=sq: e.tensor_tensor(
                    out=sq, in0=s, in1=s, op=ALU.mult))
                em.wait("tensor", S["dve"])
            for j in range(4):
                em.do("tensor", lambda e, c=c, j=j, s=src_c: e.matmul(
                    st1[:, j * 512:(j + 1) * 512], self.onesD[:],
                    s[:, j * 512:(j + 1) * 512], start=(c == 0), stop=(c == CH - 1),
                    skip_group_check=True))
            for j in range(4):
                em.do("tensor", lambda e, c=c, j=j, sq=sq: e.matmul(
                    st2[:, j * 512:(j + 1) * 512], self.onesD[:],
                    sq[:, j * 512:(j + 1) * 512], start=(c == 0), stop=(c == CH - 1),
                    skip_group_check=True))
            pe_marks.append(S["pe"].n)
        em.wait("vector", S["pe"])
        em.do("vector", lambda e: e.tensor_copy(out=self.rowb[0:1, :], in_=st1))
        em.do("vector", lambda e: e.tensor_tensor(out=self.rowb[64:65, :],
              in0=self.rowb[0:1, :], in1=self.rowb[0:1, :], op=ALU.mult))
        em.do("vector", lambda e: e.scalar_tensor_tensor(out=self.rowb[96:97, :],
              in0=st2, scalar=float(EPS), in1=self.rowb[64:65, :],
              op0=ALU.add, op1=ALU.subtract))
        self._veps_dve = S["dve"].n
        em.wait("scalar", S["dve"])
        em.do("scalar", lambda e: e.activation(out=self.rowb[64:65, :], in_=self.rowb[96:97, :], func=AF.Ln))
        em.do("scalar", lambda e: e.activation(out=self.rowb[32:33, :], in_=self.rowb[64:65, :],
              func=AF.Exp, scale=f32(-0.5)))
        em.wait("vector", S["act"])
        em.do("vector", lambda e: e.tensor_copy(out=self.lnsq[0:1, :], in_=self.rowb[32:33, :]))
        bcast = self.psA.ap()[:, 2048:4096]
        em.wait("tensor", S["dve"])
        for j in range(4):
            em.do("tensor", lambda e, j=j: e.matmul(bcast[:, j * 512:(j + 1) * 512],
                  self.ones1r[:], self.rowb[0:1, j * 512:(j + 1) * 512],
                  start=True, stop=True))
        em.wait("scalar", S["pe"])
        em.do("scalar", lambda e: e.copy(out=self.mB[:], in_=bcast))
        em.wait("tensor", S["act"])
        em.wait("tensor", S["dve"])
        for j in range(4):
            em.do("tensor", lambda e, j=j: e.matmul(bcast[:, j * 512:(j + 1) * 512],
                  self.ones1r[:], self.lnsq[0:1, j * 512:(j + 1) * 512],
                  start=True, stop=True))
        em.wait("scalar", S["pe"])
        em.do("scalar", lambda e: e.copy(out=self.rB[:], in_=bcast))
        self._ln_guard = (S["act"].n, self._veps_dve)
        em.wait("vector", S["act"])
        self.ln_apply(srcbf=srcbf, use_x32=use_x32_cast, dst=dst, mode=mode,
                      gamma=gamma, beta=beta, half=half)
        if mode == "add":
            self._psA_guard = self._ln_guard

    def ln_apply(self, srcbf=None, use_x32=False, dst=None, mode="norm",
                 gamma=None, beta=None, half=None):
        em, S = self.em, self.S
        self._add_marks = {}
        self._apply_marks = {}
        lo = 0 if half is None else half * 1024
        w = T if half is None else 1024
        em.wait("vector", S["act"])
        for c in range(CH):
            src_c = (self.xb.ap()[:, c * T + lo: c * T + lo + w] if use_x32
                     else srcbf(c)[:, lo:lo + w])
            eng = "vector"
            t0 = self.aptmp[c % 2][:, 0:w]
            em.do(eng, lambda e, s=src_c, t0=t0, lo=lo, w=w: e.tensor_tensor(
                out=t0, in0=s, in1=self.mB[:, lo:lo + w], op=ALU.subtract))
            if mode == "norm":
                em.do(eng, lambda e, c=c, t0=t0, lo=lo, w=w: e.tensor_tensor(
                    out=dst(c), in0=t0, in1=self.rB[:, lo:lo + w], op=ALU.mult))
            elif mode == "gamma":
                em.do(eng, lambda e, t0=t0, lo=lo, w=w: e.tensor_tensor(
                    out=t0, in0=t0, in1=self.rB[:, lo:lo + w], op=ALU.mult))
                em.do(eng, lambda e, c=c, t0=t0: e.tensor_scalar(
                    out=dst(c), in0=t0, scalar1=gamma[:, c:c + 1], scalar2=beta[:, c:c + 1],
                    op0=ALU.mult, op1=ALU.add))
            else:
                em.do(eng, lambda e, t0=t0, lo=lo, w=w: e.tensor_tensor(
                    out=t0, in0=t0, in1=self.rB[:, lo:lo + w], op=ALU.mult))
                x32c = self.xb.ap()[:, c * T + lo: c * T + lo + w]
                em.do(eng, lambda e, x=x32c, t0=t0: e.tensor_tensor(
                    out=x, in0=x, in1=t0, op=ALU.add))
                self._add_marks[c] = S["dve"].n
            self._apply_marks[c] = S["dve"].n
        em.wait("vector", S["pool"])
        em.wait("gpsimd", S["dve"])

    def prestream_sides(self, wname, sync_gate):
        em, S = self.em, self.S
        pe_guard = S["pe"].n
        state = {"marks": []}
        def mk_dma(i):
            def emit():
                if i < 2:
                    em.wait("sync", S["act"], sync_gate[0])
                    em.wait("sync", S["dve"], sync_gate[1])
                    em.wait("sync", S["pool"], sync_gate[2])
                else:
                    em.wait("sync", S["dve"], state["marks"][i - 2])
                st = self.stage[i % 2]
                src3 = self.ins[wname].rearrange("(c p) o -> p c o", p=128)[:, i:i + 1, :]
                em.do("sync", lambda e, st=st, src3=src3: e.dma_start(
                    out=st[:].rearrange("p (c o) -> p c o", c=1)[:, 0:1, :], in_=src3),
                    inc=S["sy"], inc_n=16)
                self.dsig()
            return emit
        def mk_cast(i):
            def emit():
                em.wait("vector", S["syd"])
                em.wait("vector", S["pe"], pe_guard)
                st = self.stage[i % 2]
                em.do("vector", lambda e, st=st, i=i: e.tensor_copy(
                    out=self.warena.ap()[:, i * 1024:(i + 1) * 1024], in_=st[:, 0:1024]))
                state["marks"].append(S["dve"].n)
                state["mark"] = S["dve"].n
            return emit
        pad = lambda: None
        slots = {0: mk_dma(0), 1: mk_dma(1)}
        for i in range(8):
            slots[7 * (i + 1)] = mk_cast(i)
            if i + 2 < 8:
                slots[7 * (i + 1) + 1] = mk_dma(i + 2)
        n = max(slots) + 1
        return [slots.get(k, pad) for k in range(n)], state

    def stream_weight(self, src_name, dst_bf_ap, n_chunks=CH, cols=1024, col0=0,
                      dst_off=0, rearr=True, pe_guard=None, cast_eng=None,
                      sync_gate=None, stage_off=0, deep_stage=False):
        em, S = self.em, self.S
        src = self.ins[src_name]
        if sync_gate is None:
            em.wait("sync", S["pool"])
            em.wait("sync", S["dve"])
        else:
            em.wait("sync", S["act"], sync_gate[0])
            em.wait("sync", S["dve"], sync_gate[1])
            em.wait("sync", S["pool"], sync_gate[2])
        ceng = cast_eng or "vector"
        if pe_guard is None:
            em.wait("gpsimd", S["pe"])
            em.wait("vector", S["pe"])
            if ceng == "scalar":
                em.wait("scalar", S["pe"])
        elif pe_guard > 0:
            em.wait("gpsimd", S["pe"], pe_guard)
            em.wait("vector", S["pe"], pe_guard)
            if ceng == "scalar":
                em.wait("scalar", S["pe"], pe_guard)
        cast_sem = {"vector": S["dve"], "scalar": S["act"]}.get(ceng, S["pool"])
        nch_per = max(1, 1024 // cols)
        i = 0
        it = 0
        cast_marks = []
        sbufs = self.stage if not deep_stage else self.stage + self.stage2nd
        nst = len(sbufs)
        while i < n_chunks:
            k = min(nch_per, n_chunks - i)
            st = sbufs[(it + stage_off) % nst]
            if it >= nst:
                sem, cnt = cast_marks[it - nst]
                em.wait("sync", sem, cnt)
            src3 = src.rearrange("(c p) o -> p c o", p=128)[:, i:i + k, col0:col0 + cols]
            em.do("sync", lambda e, st=st, src3=src3, k=k, cols=cols: e.dma_start(
                out=st[:].rearrange("p (c o) -> p c o", c=1024 // cols)[:, 0:k, :], in_=src3),
                inc=S["sy"], inc_n=16)
            self.dsig()
            em.wait(ceng, S["syd"])
            if ceng == "scalar":
                em.do(ceng, lambda e, st=st, i=i, k=k, cols=cols: e.copy(
                    out=dst_bf_ap[:, dst_off + i * cols: dst_off + (i + k) * cols],
                    in_=st[:, 0:k * cols]))
            else:
                em.do(ceng, lambda e, st=st, i=i, k=k, cols=cols: e.tensor_copy(
                    out=dst_bf_ap[:, dst_off + i * cols: dst_off + (i + k) * cols],
                    in_=st[:, 0:k * cols]))
            cast_marks.append((cast_sem, cast_sem.n))
            i += k
            it += 1

    def proj_full(self, wname, src_bf, dst_bf, bias=None, gelu=False,
                  add_from=None, src_cols=T, src_lo=0, width=None, dst_cols=None,
                  dst_lo=0, wT_dram=None, sync_gate=None, prestream=None):
        em, S = self.em, self.S
        w = width or src_cols
        dst_cols = dst_cols or src_cols
        if prestream is not None:
            em.wait("tensor", S["dve"], prestream["mark"])
        else:
            self.stream_weight(wname, self.warena.ap(), n_chunks=CH, cols=1024,
                               sync_gate=sync_gate, cast_eng="scalar")
            self.wait_w("tensor")
        em.wait("tensor", S["act"])
        em.wait("tensor", S["dve"])
        nj = (w + 511) // 512
        for oc in range(CH):
            ps = self.psA.ap()[:, (oc % 2) * 2048:(oc % 2) * 2048 + w]
            if oc >= 2:
                em.wait("tensor", S["act"])
                em.wait("tensor", S["dve"])
            for kc in range(CH):
                for j in range(nj):
                    j0, j1 = j * 512, min((j + 1) * 512, w)
                    em.do("tensor", lambda e, oc=oc, kc=kc, j0=j0, j1=j1, ps=ps: e.matmul(
                        ps[:, j0:j1],
                        self.warena.ap()[:, kc * 1024 + oc * 128: kc * 1024 + (oc + 1) * 128],
                        src_bf.ap()[:, kc * src_cols + src_lo + j0: kc * src_cols + src_lo + j1],
                        start=(kc == 0), stop=(kc == CH - 1), skip_group_check=True),
                        inc=S["pe"] if (kc == CH - 1 and j == nj - 1) else None)
            dst_ap = dst_bf.ap()[:, oc * dst_cols + dst_lo: oc * dst_cols + dst_lo + w]
            if gelu:
                em.wait("scalar", S["pe"])
                b = bias[:, oc:oc + 1] if bias is not None else f32(0.0)
                em.do("scalar", lambda e, ps=ps, b=b, d=dst_ap: e.activation(
                    out=d, in_=ps, func=AF.Gelu, bias=b), inc=S["act"])
            elif add_from is not None:
                em.wait("vector", S["pe"])
                b = bias[:, oc:oc + 1] if bias is not None else 0.0
                em.do("vector", lambda e, ps=ps, b=b, d=dst_ap, oc=oc: e.scalar_tensor_tensor(
                    out=d, in0=ps, scalar=b, in1=add_from(oc),
                    op0=ALU.add, op1=ALU.add), inc=S["dve"])
            elif bias is not None:
                em.wait("vector", S["pe"])
                em.do("vector", lambda e, ps=ps, oc=oc, d=dst_ap: e.tensor_scalar_add(
                    out=d, in0=ps, scalar1=bias[:, oc:oc + 1]), inc=S["dve"])
            else:
                em.wait("scalar", S["pe"])
                em.do("scalar", lambda e, ps=ps, d=dst_ap: e.copy(out=d, in_=ps), inc=S["act"])
        em.wait("tensor", S["dve"])
        em.wait("tensor", S["act"])

    def emit_l0a(self):
        em, S = self.em, self.S
        cosf = self.rotc.ap()[:, 0:T]
        sinf = self.rotc.ap()[:, T:2 * T]
        permT = self.rotc.ap()[:, 2 * T:2 * T + 128]
        em.do("sync", lambda e: e.dma_start(out=cosf, in_=self.ins["cosf"]), inc=S["sy"], inc_n=16)
        em.do("sync", lambda e: e.dma_start(out=sinf, in_=self.ins["sinf"]), inc=S["sy"], inc_n=16)
        em.do("sync", lambda e: e.dma_start(out=permT, in_=self.ins["permT"]), inc=S["sy"], inc_n=16)
        self.dsig()
        consts_ld = S["syd"].n

        qbuf = [self.qtp, self.scr16.ap()[:, T:2 * T]]
        kbuf = [self.kvp, self.scr16.ap()[:, 2 * T:3 * T]]
        vpar = [self.vp.ap()[:, 0:T], self.vp.ap()[:, T:2 * T]]
        pe_proj_mark = [0, 0]
        if not hasattr(self, "_side_bank_mark"):
            self._side_bank_mark = {0: 0, 1: 0}

        wf32 = self.warena.ap().bitcast(F32)

        def l0a_weights_dma(hp, dve_gate):
            def emit():
                em.wait("sync", S["dve"], dve_gate)
                for (wname, scr) in (("wq0", wf32[:, 1024:2048]), ("wkv0", wf32[:, 3072:4096])):
                    src3 = self.ins[wname].rearrange("(c p) o -> p c o", p=128)[:, :, hp * 128:(hp + 1) * 128]
                    em.do("sync", lambda e, scr=scr, src3=src3: e.dma_start(
                        out=scr.rearrange("p (c o) -> p c o", c=8), in_=src3), inc=S["sy"], inc_n=16)
                    self.dsig()
            return emit

        def l0a_weights_cast(hp, pe_guard):
            slot = (hp % 2) * 4096
            wq_p = self.warena.ap()[:, slot:slot + 1024]
            wkv_p = self.warena.ap()[:, slot + 1024:slot + 2048]
            marks = {}
            em.wait("vector", S["syd"])
            em.wait("vector", S["pe"], pe_guard)
            for (wname, scr, dstw) in (("wq0", wf32[:, 1024:2048], wq_p),
                                       ("wkv0", wf32[:, 3072:4096], wkv_p)):
                em.do("vector", lambda e, scr=scr, d=dstw: e.tensor_copy(out=d, in_=scr))
                marks[wname] = (S["dve"].n, S["pool"].n)
            return wq_p, wkv_p, marks

        def proj_sides(wsrc, dstp, marks, wname):
            ops = []
            for j in range(4):
                pb = self.psA.ap()[:, 3072 + (j % 2) * 512: 3072 + (j % 2) * 512 + 512]
                def mk_mm(kc0, j=j, pb=pb, first=(j == 0)):
                    def emit():
                        if first and kc0 == 0:
                            dn, pn = marks[wname]
                            em.wait("tensor", S["dve"], dn)
                            em.wait("tensor", S["pool"], pn)
                        if kc0 == 0:
                            em.wait("tensor", S["dve"], self._side_bank_mark[j % 2])
                        for kc in (kc0, kc0 + 1):
                            em.do("tensor", lambda e, kc=kc, j=j, pb=pb: e.matmul(
                                pb, wsrc[:, kc * 128:(kc + 1) * 128],
                                self.xn.ap()[:, kc * T + j * 512: kc * T + (j + 1) * 512],
                                start=(kc == 0), stop=(kc == CH - 1), skip_group_check=True))
                    return emit
                for kc0 in (0, 2, 4, 6):
                    ops.append(mk_mm(kc0))
                def mk_evac(j=j, pb=pb):
                    def emit():
                        em.wait("vector", S["pe"])
                        em.do("vector", lambda e, j=j, pb=pb: e.tensor_copy(
                            out=dstp[:, j * 512:(j + 1) * 512], in_=pb))
                        self._side_bank_mark[j % 2] = S["dve"].n
                    return emit
                ops.append(mk_evac())
            return ops

        def rot_sides(buf):
            ops = []
            for j in range(4):
                pb = self.psA.ap()[:, 3072 + (j % 2) * 512: 3072 + (j % 2) * 512 + 512]
                sl = slice(j * 512, (j + 1) * 512)
                rot_sl = self.mB.ap()[:, sl]
                t2_sl = self.rB.ap()[:, sl]
                def mk_rotmm(j=j, pb=pb, sl=sl):
                    def emit():
                        em.wait("tensor", S["dve"], self._side_bank_mark[j % 2])
                        em.do("tensor", lambda e, pb=pb, sl=sl: e.matmul(
                            pb, permT, buf[:, sl], start=True, stop=True))
                    return emit
                ops.append(mk_rotmm())
                def mk_evac(j=j, pb=pb, rot_sl=rot_sl):
                    def emit():
                        em.wait("vector", S["pe"])
                        em.do("vector", lambda e, pb=pb, rot_sl=rot_sl: e.tensor_copy(
                            out=rot_sl, in_=pb))
                        self._side_bank_mark[j % 2] = S["dve"].n
                    return emit
                ops.append(mk_evac())
                def mk_tt(j=j, sl=sl, rot_sl=rot_sl, t2_sl=t2_sl):
                    def emit():
                        em.wait("vector", S["pe"])
                        em.do("vector", lambda e, sl=sl, t2_sl=t2_sl: e.tensor_tensor(
                            out=t2_sl, in0=buf[:, sl], in1=cosf[:, sl], op=ALU.mult))
                        em.do("vector", lambda e, sl=sl, rot_sl=rot_sl: e.tensor_tensor(
                            out=rot_sl, in0=rot_sl, in1=sinf[:, sl], op=ALU.mult))
                        em.do("vector", lambda e, sl=sl, rot_sl=rot_sl, t2_sl=t2_sl: e.tensor_tensor(
                            out=buf[:, sl], in0=rot_sl, in1=t2_sl, op=ALU.add))
                    return emit
                ops.append(mk_tt())
            return ops

        def proj_inline(wsrc, dstp, marks=None):
            em.wait("tensor", S["pool"])
            if marks is None:
                em.wait("tensor", S["dve"])
            else:
                em.wait("tensor", S["dve"], getattr(self, "_ps_evac_mark", 0))
            em.wait("tensor", S["act"])
            ps = self.psA.ap()[:, 0:2048]
            for kc in range(CH):
                if marks is not None:
                    em.wait("tensor", S["dve"], marks[kc])
                for j in range(4):
                    em.do("tensor", lambda e, kc=kc, j=j, w=wsrc, ps=ps: e.matmul(
                        ps[:, j * 512:(j + 1) * 512], w[:, kc * 128:(kc + 1) * 128],
                        self.xn.ap()[:, kc * T + j * 512: kc * T + (j + 1) * 512],
                        start=(kc == 0), stop=(kc == CH - 1), skip_group_check=True))
            em.wait("vector", S["pe"])
            em.do("vector", lambda e, d=dstp, ps=ps: e.tensor_copy(out=d[:, :], in_=ps))
            self._ps_evac_mark = S["dve"].n

        def rotary_inline(buf):
            rp = self.psA.ap()[:, 2048:4096]
            em.wait("tensor", S["dve"])
            em.wait("tensor", S["act"])
            for j in range(4):
                em.do("tensor", lambda e, j=j, buf=buf, rp=rp: e.matmul(
                    rp[:, j * 512:(j + 1) * 512], permT, buf[:, j * 512:(j + 1) * 512],
                    start=True, stop=True))
            em.wait("scalar", S["pe"])
            rot_bf = self.pt[0]
            em.do("scalar", lambda e, r=rot_bf, rp=rp: e.copy(out=r[:, :], in_=rp))
            t2 = self.pt[1]
            em.wait("vector", S["pe"])
            em.wait("vector", S["act"])
            em.do("vector", lambda e, buf=buf, t2=t2: e.tensor_tensor(
                out=t2[:, :], in0=buf, in1=cosf, op=ALU.mult))
            em.do("vector", lambda e, r=rot_bf: e.tensor_tensor(
                out=r[:, :], in0=r[:, :], in1=sinf, op=ALU.mult))
            em.do("vector", lambda e, r=rot_bf, t2=t2, buf=buf: e.tensor_tensor(
                out=buf, in0=r[:, :], in1=t2[:, :], op=ALU.add))

        def vtrans_side(buf, dst):
            def emit():
                em.wait("sync", S["dve"])
                em.wait("sync", S["pe"])
                for tt in range(16):
                    em.do("sync", lambda e, tt=tt, buf=buf, dst=dst: e.dma_start_transpose(
                        out=dst[:, tt * 128:(tt + 1) * 128],
                        in_=buf[:, tt * 128:(tt + 1) * 128]), inc=S["sy"], inc_n=16)
                self.dsig()
                self._vp_gate = S["syd"].n
            return emit

        def vtrans(buf, dst):
            em.wait("sync", S["dve"])
            em.wait("sync", S["pe"])
            for tt in range(16):
                em.do("sync", lambda e, tt=tt, buf=buf, dst=dst: e.dma_start_transpose(
                    out=dst[:, tt * 128:(tt + 1) * 128],
                    in_=buf[:, tt * 128:(tt + 1) * 128]), inc=S["sy"], inc_n=16)
            self.dsig()
            self._vp_gate = S["syd"].n

        l0a_weights_dma(0, 0)()
        wq0, wkv0, marks0 = l0a_weights_cast(0, 0)
        l0a_weights_dma(1, S["dve"].n)()
        self.emit_ln(use_x32_cast=True, dst=self.chunks(self.xn), mode="norm",
                     src_marks=getattr(self, "_add_marks", None))
        self.dump_full("xn", self.xn)
        xnm0 = dict(self._apply_marks)
        em.wait("gpsimd", S["syd"], consts_ld)
        em.wait("vector", S["syd"], consts_ld)
        proj_inline(wq0, qbuf[0], marks=xnm0)
        rotary_inline(qbuf[0][:, :])
        proj_inline(wkv0, kbuf[0], marks=xnm0)
        rotary_inline(kbuf[0][:, :])
        pe_proj_mark[0] = S["pe"].n
        vtrans(kbuf[0][:, :], vpar[0])
        rot_mark = S["dve"].n
        for hp in range(NP):
            if hp + 1 < NP:
                wqn, wkvn, marksn = l0a_weights_cast(hp + 1, pe_proj_mark[(hp + 1) % 2])
                self._side = (proj_sides(wqn, qbuf[(hp + 1) % 2], marksn, "wq0")
                              + rot_sides(qbuf[(hp + 1) % 2])
                              + proj_sides(wkvn, kbuf[(hp + 1) % 2], marksn, "wkv0")
                              + rot_sides(kbuf[(hp + 1) % 2])
                              + [vtrans_side(kbuf[(hp + 1) % 2][:, :], vpar[(hp + 1) % 2])])
                if hp + 2 < NP:
                    self._side = self._side + [l0a_weights_dma(hp + 2, S["dve"].n)]
            else:
                wops, self._wout0_pre = self.prestream_sides("wout0", self._stage_gate)
                self._side = wops
            self._qsrc = qbuf[hp % 2]
            self._ksrc = kbuf[hp % 2]
            self._vsrc = vpar[hp % 2]
            em.wait("tensor", S["dve"], rot_mark)
            if hp == 0:
                self.dump_one("qt", self.qtp[:, :])
                self.dump_one("kv", self.kvp[:, :])
                self.dump_one("vp", self.vp.ap()[:, :])
            self._attn_loop(hp, causal=False)
            if hp + 1 < NP:
                rot_mark = S["dve"].n
            pe_proj_mark[(hp + 1) % 2] = S["pe"].n
        self._qsrc = None
        self._ksrc = None
        self._vsrc = None
        self.proj_full("wout0", self.outT, self.xn, bias=self.bias["bout0"],
                       prestream=self._wout0_pre)
        self._stage_gate2 = (S["act"].n, S["dve"].n, S["pool"].n)
        self.dump_full("z", self.xn)
        self.emit_ln(srcbf=self.chunks(self.xn), mode="add")

    def _attn_loop(self, hp, causal):
        em, S = self.em, self.S
        Sps = [self.psA.ap()[:, i * 1024:(i + 1) * 1024] for i in range(2)]
        AVps = self.psA.ap()[:, 2048:2560]
        Lps = self.psA.ap()[0:64, 2560:3072]
        qsrc = getattr(self, "_qsrc", None) or self.qtp
        ksrc = getattr(self, "_ksrc", None) or self.kvp
        side = getattr(self, "_side", None) or []
        self._side = None
        side_i = 0
        QB = 512
        DEPTH = 2
        self._ptv = [self.pt[0][:, 0:1024], self.pt[0][:, 1024:2048], self.pt[1][:, 0:1024]]
        prev_exp_end = getattr(self, "_attn_prev_exp_end", 0)
        prev_fin_dve = getattr(self, "_attn_prev_fin_dve", 0)
        prev_fin_act = getattr(self, "_attn_prev_fin_act", 0)
        for qb in range(0, 2048, QB):
            kc_max = 16 if not causal else (qb + QB) // 128
            exp_cnt = []
            em.wait("tensor", S["act"], prev_exp_end)
            fin_dve_gate = prev_fin_dve
            fin_act_gate = prev_fin_act
            for kc in range(kc_max):
                off = kc * 128 - qb if causal else -1
                lo = max(0, off)
                Sb = Sps[kc % 2]
                pt = self._ptv[kc % 3]
                if kc >= 2:
                    em.wait("tensor", S["act"], exp_cnt[kc - 2])
                for (hh, rlo, tp) in ((0, 0, (0, 0)), (1, 64, (64, 0))):
                    em.do("tensor", lambda e, hh=hh, rlo=rlo, tp=tp, lo=lo, kc=kc, qb=qb, Sb=Sb: e.matmul(
                        Sb[:, hh * 512 + lo: (hh + 1) * 512],
                        ksrc[rlo:rlo + 64, kc * 128:(kc + 1) * 128],
                        qsrc[rlo:rlo + 64, qb + lo: qb + 512],
                        start=True, stop=True, tile_position=tp, skip_group_check=True))
                pe_after_S = S["pe"].n
                if kc == 1:
                    em.wait("tensor", S["dve"], fin_dve_gate)
                    em.wait("tensor", S["act"], fin_act_gate)
                if kc >= 1:
                    self._avl512(hp, qb, kc - 1, kc_max, causal, AVps, Lps)
                em.wait("scalar", S["pe"], pe_after_S)
                em.do("scalar", lambda e, pt=pt, Sb=Sb: e.activation(
                    out=pt[:, 0:1024], in_=Sb, func=AF.Exp, bias=SHIFT, scale=SCALE8))
                exp_cnt.append(S["act"].n)
                if causal and 0 <= off < QB:
                    em.wait("vector", S["act"], exp_cnt[kc])
                    for hh in range(2):
                        em.do("vector", lambda e, hh=hh, off=off, pt=pt: e.tensor_tensor(
                            out=pt[:, hh * 512 + off: hh * 512 + off + 128],
                            in0=pt[:, hh * 512 + off: hh * 512 + off + 128],
                            in1=self.tril[:], op=ALU.mult))
                for _ in range(3):
                    if side_i < len(side):
                        side[side_i]()
                        side_i += 1
            self._avl512(hp, qb, kc_max - 1, kc_max, causal, AVps, Lps, last=True)
            prev_exp_end = exp_cnt[-1]
            rlB = self.psA.ap()[:, 2560:3072]
            rf = self.rowb.ap().bitcast(F32)
            em.wait("scalar", S["pe"])
            em.do("scalar", lambda e, rf=rf: e.activation(out=rf[0:33, 0:512], in_=Lps[0:33, 0:512], func=AF.Ln))
            em.do("scalar", lambda e, rf=rf: e.activation(out=rf[0:33, 0:512], in_=rf[0:33, 0:512], func=AF.Exp, scale=f32(-1.0)))
            em.wait("vector", S["act"])
            em.do("vector", lambda e, rf=rf: e.tensor_copy(out=self.rowb2[0:1, 0:512], in_=rf[0:1, 0:512]))
            em.do("vector", lambda e, rf=rf: e.tensor_copy(out=self.rowb2[32:33, 0:512], in_=rf[32:33, 0:512]))
            em.wait("tensor", S["dve"])
            em.wait("tensor", S["act"])
            em.do("tensor", lambda e, rlB=rlB: e.matmul(rlB, self.e33[:], self.rowb2[0:33, 0:512],
                  start=True, stop=True, skip_group_check=True))
            em.wait("vector", S["pe"])
            em.do("vector", lambda e, rlB=rlB: e.tensor_copy(out=self.lncast[:, 0:512], in_=rlB))
            em.do("vector", lambda e, hp=hp, qb=qb: e.tensor_tensor(
                out=self.outT.ap()[:, hp * T + qb: hp * T + qb + 512],
                in0=AVps, in1=self.lncast[:, 0:512], op=ALU.mult))
            prev_fin_dve = S["dve"].n
            prev_fin_act = S["act"].n
        while side_i < len(side):
            side[side_i]()
            side_i += 1
        self._attn_prev_exp_end = prev_exp_end
        self._attn_prev_fin_dve = prev_fin_dve
        self._attn_prev_fin_act = prev_fin_act

    def _avl512(self, hp, qb, kc, kc_max, causal, AVps, Lps, last=False):
        em, S = self.em, self.S
        off = kc * 128 - qb if causal else -1
        lo = max(0, off)
        if kc == 0 and getattr(self, "_vp_gate", 0):
            em.wait("tensor", S["syd"], self._vp_gate)
        if causal and 0 <= off < 512:
            em.wait("tensor", S["dve"])
        else:
            em.wait("tensor", S["act"])
        pt = self._ptv[kc % 3]
        vsrc = getattr(self, "_vsrc", None)
        if vsrc is None:
            vsrc = self.vp.ap()[:, :]
        for (hh, tpav) in ((0, (0, 0)), (1, (0, 64))):
            em.do("tensor", lambda e, hh=hh, tpav=tpav, lo=lo, kc=kc, pt=pt, vsrc=vsrc: e.matmul(
                AVps[hh * 64:(hh + 1) * 64, lo:512],
                vsrc[:, kc * 128 + hh * 64: kc * 128 + hh * 64 + 64],
                pt[:, hh * 512 + lo: (hh + 1) * 512],
                start=(kc == 0), stop=(kc == kc_max - 1),
                tile_position=tpav, skip_group_check=True))
        for (hh, tpl) in ((0, (0, 0)), (1, (0, 32))):
            em.do("tensor", lambda e, hh=hh, tpl=tpl, lo=lo, kc=kc, pt=pt: e.matmul(
                Lps[hh * 32:hh * 32 + 32, lo:512],
                self.onespad[:], pt[:, hh * 512 + lo: (hh + 1) * 512],
                start=(kc == 0), stop=(kc == kc_max - 1),
                tile_position=tpl, skip_group_check=True))

    def emit_store(self):
        em, S = self.em, self.S
        marks = getattr(self, "_add_marks", {})
        em.wait("scalar", S["pool"])
        for c in range(CH):
            em.wait("scalar", S["dve"], marks.get(c, S["dve"].n))
            if c >= 1:
                em.wait("scalar", S["syd"])
            for hf in range(2):
                em.do("scalar", lambda e, c=c, hf=hf: e.copy(
                    out=self.stage[hf][:],
                    in_=self.xb.ap()[:, c * T + hf * 1024: c * T + (hf + 1) * 1024]))
            em.wait("sync", S["act"])
            for hf in range(2):
                em.do("sync", lambda e, c=c, hf=hf: e.dma_start(
                    out=self.x_out[c * 128:(c + 1) * 128, hf * 1024:(hf + 1) * 1024],
                    in_=self.stage[hf][:]), inc=S["sy"], inc_n=16)
            self.dsig()
        em.wait("sync", S["sy"])

    def emit_l0f(self):
        em, S = self.em, self.S
        P = self.pairs.ap()
        atbf = P[:, 0:1024]
        btbf = P[:, 1024:2048]
        rtbf_cols = 4608
        rtbf = P[:, rtbf_cols:rtbf_cols + 64]
        lgbf = P[:, 2048:4096]
        gtbf = P[:, 6144:8192]
        hbf_cols = 0
        self.stream_weight("aT", atbf, n_chunks=CH, cols=128,
                           sync_gate=self._stage_gate2, stage_off=0, cast_eng="scalar")
        self.stream_weight("bT", btbf, n_chunks=1, cols=1024,
                           sync_gate=self._stage_gate2, stage_off=1, cast_eng="scalar")
        self.stream_weight("rT", rtbf, n_chunks=CH, cols=8, cast_eng="scalar",
                           sync_gate=(S["act"].n, S["dve"].n, S["pool"].n))
        self._l0f_gate = (S["act"].n, S["dve"].n, S["pool"].n)
        em.wait("tensor", S["act"])
        self.emit_ln(use_x32_cast=True, dst=self.chunks(self.xn), mode="norm",
                     src_marks=getattr(self, "_add_marks", None),
                     sq_eng="scalar", sq_scratch=self.chunks(self.xn))
        xnm = dict(self._apply_marks)
        lg_ps = self.psA.ap()[0:8, 2048:4096]
        for kc in range(CH):
            em.wait("tensor", S["dve"], xnm[kc])
            for j in range(4):
                em.do("tensor", lambda e, kc=kc, j=j: e.matmul(
                    lg_ps[:, j * 512:(j + 1) * 512], rtbf[:, kc * 8:(kc + 1) * 8],
                    self.xn.ap()[:, kc * T + j * 512:kc * T + (j + 1) * 512],
                    start=(kc == 0), stop=(kc == CH - 1), skip_group_check=True),
                    inc=S["pe"] if (kc == CH - 1 and j == 3) else None)
        em.wait("scalar", S["pe"])
        em.do("scalar", lambda e: e.copy(out=lgbf[0:8, :], in_=lg_ps))
        h_ps = self.psA.ap()[:, 0:2048]
        for kc in range(CH):
            for j in range(4):
                em.do("tensor", lambda e, kc=kc, j=j: e.matmul(
                    h_ps[:, j * 512:(j + 1) * 512], atbf[:, kc * 128:(kc + 1) * 128],
                    self.xn.ap()[:, kc * T + j * 512:kc * T + (j + 1) * 512],
                    start=(kc == 0), stop=(kc == CH - 1), skip_group_check=True),
                    inc=S["pe"] if (kc == CH - 1 and j == 3) else None)
        em.wait("vector", S["pe"])
        hbf = self.pt[0]
        em.do("vector", lambda e, hbf=hbf: e.tensor_copy(out=hbf[:, :], in_=h_ps))
        hbf_mark = S["dve"].n
        rbf32 = self.rowb.ap().bitcast(F32)
        gl = rbf32[:, 0:128]
        em.wait("tensor", S["act"])
        for tt in range(16):
            pso = self.psA.ap()[:, 2048 + (tt % 2) * 512: 2048 + (tt % 2) * 512 + 4].bitcast(BF16)
            if tt >= 2:
                em.wait("tensor", S["dve"])
            em.do("tensor", lambda e, tt=tt, pso=pso: e.transpose(
                pso, lgbf[0:8, tt * 128:(tt + 1) * 128], self.ident[0:8, 0:8]),
                inc=S["pe"])
            em.wait("vector", S["pe"])
            em.do("vector", lambda e, tt=tt, pso=pso: e.tensor_copy(
                out=gl[:, tt * 8:(tt + 1) * 8], in_=pso))
        gl3 = gl.rearrange("p (a b) -> p a b", b=8)
        mx8 = rbf32[:, 128:256]
        for tt in range(16):
            em.do("vector", lambda e, tt=tt: e.max(out=mx8[:, tt * 8:(tt + 1) * 8],
                  in_=gl[:, tt * 8:(tt + 1) * 8]))
        mx83 = mx8.rearrange("p (a b) -> p a b", b=8)
        tl = rbf32[:, 256:384]
        tl3 = tl.rearrange("p (a b) -> p a b", b=8)
        em.do("vector", lambda e: e.tensor_tensor(out=tl3, in0=gl3,
              in1=mx83[:, :, 0:1].broadcast_to([128, 16, 8]), op=ALU.subtract))
        g = rbf32[:, 384:512]
        em.wait("scalar", S["dve"])
        em.do("scalar", lambda e: e.activation(out=g, in_=tl, func=AF.Exp))
        g3 = g.rearrange("p (a b) -> p a b", b=8)
        s1 = rbf32[:, 512:528]
        em.wait("vector", S["act"])
        em.do("vector", lambda e: e.tensor_reduce(out=s1, in_=g3, axis=mybir.AxisListType.X, op=ALU.add))
        keep = rbf32[:, 256:384]
        keep3 = tl3
        em.do("vector", lambda e: e.tensor_tensor(out=keep3, in0=gl3,
              in1=mx83[:, :, 1:2].broadcast_to([128, 16, 8]), op=ALU.is_ge))
        gk3 = g3
        em.do("vector", lambda e: e.tensor_tensor(out=gk3, in0=g3, in1=keep3, op=ALU.mult))
        sm = rbf32[:, 528:544]
        em.do("vector", lambda e: e.tensor_reduce(out=sm, in_=gk3, axis=mybir.AxisListType.X, op=ALU.add))
        r1 = rbf32[:, 544:560]
        em.do("vector", lambda e: e.reciprocal(out=r1, in_=s1))
        t2 = rbf32[:, 560:576]
        em.do("vector", lambda e: e.tensor_tensor(out=t2, in0=sm, in1=r1, op=ALU.mult))
        em.do("vector", lambda e: e.tensor_scalar_add(out=t2, in0=t2, scalar1=1e-6))
        rt = rbf32[:, 576:592]
        em.do("vector", lambda e: e.reciprocal(out=rt, in_=t2))
        sc = rbf32[:, 592:608]
        em.do("vector", lambda e: e.tensor_tensor(out=sc, in0=r1, in1=rt, op=ALU.mult))
        sc3 = sc.rearrange("p (a b) -> p a b", b=1)
        em.do("vector", lambda e: e.tensor_tensor(
            out=self.gate_bf.ap().rearrange("p (a b) -> p a b", b=8), in0=gk3,
            in1=sc3.broadcast_to([128, 16, 8]), op=ALU.mult), inc=S["dve"])
        self.dump_one("gate", self.gate_bf.ap()[:, :])
        em.wait("tensor", S["dve"])
        for tt in range(16):
            pso = self.psA.ap()[:, 2048 + (tt % 2) * 512: 2048 + (tt % 2) * 512 + 64].bitcast(BF16)
            if tt >= 2:
                em.wait("tensor", S["dve"])
            em.do("tensor", lambda e, tt=tt, pso=pso: e.transpose(
                pso[0:8, :], self.gate_bf.ap()[:, tt * 8:(tt + 1) * 8], self.ident[:]),
                inc=S["pe"])
            em.wait("vector", S["pe"])
            em.do("vector", lambda e, tt=tt, pso=pso: e.tensor_copy(
                out=gtbf[0:8, tt * 128:(tt + 1) * 128], in_=pso[0:8, :]), inc=S["dve"])
        em.wait("tensor", S["dve"])
        gB = self.psA.ap()[:, 2048:4096]
        for j in range(4):
            em.do("tensor", lambda e, j=j: e.matmul(gB[:, j * 512:(j + 1) * 512],
                  self.selER[:], gtbf[0:8, j * 512:(j + 1) * 512], start=True, stop=True,
                  skip_group_check=True), inc=S["pe"] if j == 3 else None)
        em.wait("vector", S["pe"])
        ghbf = self.pt[1]
        em.do("vector", lambda e, ghbf=ghbf: e.tensor_tensor(out=ghbf[:, :], in0=self.pt[0][:, :],
              in1=gB, op=ALU.mult))
        gh_mark = S["dve"].n
        self.dump_one("h0", self.pt[0][:, :])
        self.dump_one("gh", ghbf[:, :])
        self.stream_weight("w0T", self.warena.ap(), n_chunks=CH, cols=1024,
                           cast_eng="scalar", sync_gate=self._l0f_gate, stage_off=1)
        em.wait("tensor", S["act"])
        for oc in range(CH):
            ps = self.psA.ap()[:, (oc % 2) * 2048:(oc % 2) * 2048 + 2048]
            if oc == 0:
                em.wait("tensor", S["dve"], hbf_mark)
            elif oc == 1:
                em.wait("tensor", S["dve"], gh_mark)
            else:
                em.wait("tensor", S["dve"])
            for kc in range(CH):
                for j in range(4):
                    em.do("tensor", lambda e, oc=oc, kc=kc, j=j, ps=ps: e.matmul(
                        ps[:, j * 512:(j + 1) * 512],
                        self.warena.ap()[:, kc * 1024 + oc * 128: kc * 1024 + (oc + 1) * 128],
                        self.xn.ap()[:, kc * T + j * 512: kc * T + (j + 1) * 512],
                        start=(kc == 0), stop=False, skip_group_check=True))
            if oc == 0:
                em.wait("tensor", S["dve"], gh_mark)
            for j in range(4):
                em.do("tensor", lambda e, oc=oc, j=j, ps=ps, ghbf=ghbf: e.matmul(
                    ps[:, j * 512:(j + 1) * 512],
                    btbf[:, oc * 128:(oc + 1) * 128],
                    ghbf[:, j * 512:(j + 1) * 512],
                    start=False, stop=True, skip_group_check=True),
                    inc=S["pe"] if j == 3 else None)
            em.wait("vector", S["pe"])
            em.do("vector", lambda e, oc=oc, ps=ps: e.tensor_scalar_add(
                out=self.outT.ap()[:, oc * T:(oc + 1) * T], in0=ps,
                scalar1=self.bias["b0"][:, oc:oc + 1]), inc=S["dve"])
        em.wait("tensor", S["dve"])
        if "l1a" in self.blocks:
            self.l1a_prefetch0()
        self.dump_full("y", self.outT)
        self.emit_ln(srcbf=self.chunks(self.outT), mode="add")

    def l1a_scratch(self):
        rotf32 = self.rotc.ap().bitcast(F32)
        wf32 = self.warena.ap().bitcast(F32)
        return (("wq1", rotf32[:, 0:1024], None),
                ("wk1", rotf32[:, 1024:2048], None),
                ("wv1", wf32[:, 1536:2048], wf32[:, 3584:4096]))

    def l1a_prefetch0(self):
        em, S = self.em, self.S
        em.wait("sync", S["pe"])
        em.wait("sync", S["dve"])
        em.wait("sync", S["pool"])
        em.wait("sync", S["act"])
        for (wname, scrA, scrB) in self.l1a_scratch():
            src3 = self.ins[wname].rearrange("(c p) o -> p c o", p=128)[:, :, 0:128]
            if scrB is None:
                em.do("sync", lambda e, scr=scrA, src3=src3: e.dma_start(
                    out=scr.rearrange("p (c o) -> p c o", c=8), in_=src3), inc=S["sy"], inc_n=16)
                self.dsig()
            else:
                for (scr, k0) in ((scrA, 0), (scrB, 4)):
                    em.do("sync", lambda e, scr=scr, src3=src3, k0=k0: e.dma_start(
                        out=scr.rearrange("p (c o) -> p c o", c=4),
                        in_=src3[:, k0:k0 + 4, :]), inc=S["sy"], inc_n=16)
                    self.dsig()
        self._l1a_prefetched = True

    def emit_l1a(self):
        em, S = self.em, self.S
        self._l1a_gate = (S["act"].n, S["dve"].n, S["pool"].n)
        qbuf = [self.qtp, self.scr16.ap()[:, T:2 * T]]
        kbuf = [self.kvp, self.scr16.ap()[:, 2 * T:3 * T]]
        vbuf = [self.vp.ap()[:, 0:T], self.vp.ap()[:, T:2 * T]]
        self.vp2 = None
        pe_proj_mark = [0, 0]

        l1a_scr = self.l1a_scratch()

        def l1a_weights_dma(hp, dve_gate):
            def emit():
                em.wait("sync", S["dve"], dve_gate)
                for (wname, scrA, scrB) in l1a_scr:
                    src3 = self.ins[wname].rearrange("(c p) o -> p c o", p=128)[:, :, hp * 128:(hp + 1) * 128]
                    if scrB is None:
                        em.do("sync", lambda e, scr=scrA, src3=src3: e.dma_start(
                            out=scr.rearrange("p (c o) -> p c o", c=8), in_=src3),
                            inc=S["sy"], inc_n=16)
                        self.dsig()
                    else:
                        for (scr, k0) in ((scrA, 0), (scrB, 4)):
                            em.do("sync", lambda e, scr=scr, src3=src3, k0=k0: e.dma_start(
                                out=scr.rearrange("p (c o) -> p c o", c=4),
                                in_=src3[:, k0:k0 + 4, :]), inc=S["sy"], inc_n=16)
                            self.dsig()
            return emit

        def l1a_weights_cast(hp, pe_guard):
            slot = (hp % 2) * 4096
            wq_p = self.warena.ap()[:, slot:slot + 1024]
            wk_p = self.warena.ap()[:, slot + 1024:slot + 2048]
            wv_p = self.warena.ap()[:, slot + 2048:slot + 3072]
            marks = {}
            em.wait("vector", S["syd"])
            em.wait("vector", S["pe"], pe_guard)
            for (wname, scrA, scrB), dstw in zip(l1a_scr, (wq_p, wk_p, wv_p)):
                if scrB is None:
                    em.do("vector", lambda e, scr=scrA, d=dstw: e.tensor_copy(out=d, in_=scr))
                else:
                    em.do("vector", lambda e, scr=scrA, d=dstw: e.tensor_copy(out=d[:, 0:512], in_=scr))
                    em.do("vector", lambda e, scr=scrB, d=dstw: e.tensor_copy(out=d[:, 512:1024], in_=scr))
                marks[wname] = (S["dve"].n, S["pool"].n)
            return wq_p, wk_p, wv_p, marks

        def proj_inline(hp, wsrc, dstp, bn, marks=None):
            em.wait("tensor", S["pool"])
            if marks is None:
                em.wait("tensor", S["dve"])
            else:
                em.wait("tensor", S["dve"], getattr(self, "_ps_evac_mark", 0))
            em.wait("tensor", S["act"])
            ps = self.psA.ap()[:, 0:2048]
            for kc in range(CH):
                if marks is not None:
                    em.wait("tensor", S["dve"], marks[kc])
                for j in range(4):
                    em.do("tensor", lambda e, kc=kc, j=j, w=wsrc, ps=ps: e.matmul(
                        ps[:, j * 512:(j + 1) * 512], w[:, kc * 128:(kc + 1) * 128],
                        self.xn.ap()[:, kc * T + j * 512: kc * T + (j + 1) * 512],
                        start=(kc == 0), stop=(kc == CH - 1), skip_group_check=True))
            em.wait("vector", S["pe"])
            em.do("vector", lambda e, d=dstp, ps=ps, bn=bn, hp=hp: e.tensor_scalar_add(
                out=d[:, :], in0=ps, scalar1=self.bias[bn][:, hp:hp + 1]))
            self._ps_evac_mark = S["dve"].n

        if not hasattr(self, "_side_bank_mark"):
            self._side_bank_mark = {0: 0, 1: 0}

        def proj_sides(hp, wsrc, dstp, bn, marks, wname):
            ops = []
            for j in range(4):
                pb = self.psA.ap()[:, 3072 + (j % 2) * 512: 3072 + (j % 2) * 512 + 512]
                def mk_mm(kc0, j=j, pb=pb, first=(j == 0)):
                    def emit():
                        if first and kc0 == 0:
                            dn, pn = marks[wname]
                            em.wait("tensor", S["dve"], dn)
                            em.wait("tensor", S["pool"], pn)
                        if kc0 == 0:
                            em.wait("tensor", S["dve"], self._side_bank_mark[j % 2])
                        for kc in (kc0, kc0 + 1):
                            em.do("tensor", lambda e, kc=kc, j=j, pb=pb: e.matmul(
                                pb, wsrc[:, kc * 128:(kc + 1) * 128],
                                self.xn.ap()[:, kc * T + j * 512: kc * T + (j + 1) * 512],
                                start=(kc == 0), stop=(kc == CH - 1), skip_group_check=True))
                    return emit
                for kc0 in (0, 2, 4, 6):
                    ops.append(mk_mm(kc0))
                def mk_evac(j=j, pb=pb):
                    def emit():
                        em.wait("vector", S["pe"])
                        em.do("vector", lambda e, j=j, pb=pb: e.tensor_scalar_add(
                            out=dstp[:, j * 512:(j + 1) * 512], in0=pb,
                            scalar1=self.bias[bn][:, hp:hp + 1]))
                        self._side_bank_mark[j % 2] = S["dve"].n
                    return emit
                ops.append(mk_evac())
            return ops

        def vprod_sides(hp, wv_p, marks):
            ops = []
            vpd = self.vp2
            for tt in range(16):
                pb = self.psA.ap()[:, 3072 + (tt % 2) * 512: 3072 + (tt % 2) * 512 + 128]
                def mk_mm(kc0, tt=tt, pb=pb):
                    def emit():
                        if tt == 0 and kc0 == 0:
                            dn, pn = marks["wv1"]
                            em.wait("tensor", S["dve"], dn)
                            em.wait("tensor", S["pool"], pn)
                        if kc0 == 0:
                            em.wait("tensor", S["dve"], self._side_bank_mark[tt % 2])
                        for kc in range(kc0, min(kc0 + 3, CH)):
                            em.do("tensor", lambda e, tt=tt, kc=kc, pb=pb: e.matmul(
                                pb, self.xn.ap()[:, kc * T + tt * 128: kc * T + (tt + 1) * 128],
                                wv_p[:, kc * 128:(kc + 1) * 128],
                                start=(kc == 0), stop=False, skip_group_check=True))
                        if kc0 + 3 >= CH:
                            em.do("tensor", lambda e, tt=tt, pb=pb, hp=hp: e.matmul(
                                pb, self.ones1r[:], self.bvrow[0:1, hp * 128:(hp + 1) * 128],
                                start=False, stop=True, skip_group_check=True))
                    return emit
                for kc0 in (0, 3, 6):
                    ops.append(mk_mm(kc0))
                def mk_evac(tt=tt, pb=pb, vpd=vpd):
                    def emit():
                        em.wait("vector", S["pe"])
                        em.do("vector", lambda e, tt=tt, pb=pb, vpd=vpd: e.tensor_copy(
                            out=vpd[:, tt * 128:(tt + 1) * 128], in_=pb))
                        self._side_bank_mark[tt % 2] = S["dve"].n
                    return emit
                ops.append(mk_evac())
            return ops

        def vprod_inline(hp, wv_p, marks=None):
            if marks is None:
                em.wait("tensor", S["dve"])
            em.wait("tensor", S["act"])
            for tt in range(16):
                psv = self.psA.ap()[:, 2048 + (tt % 2) * 512: 2048 + (tt % 2) * 512 + 128]
                if tt >= 2:
                    em.wait("tensor", S["dve"])
                for kc in range(CH):
                    if marks is not None and tt == 0:
                        em.wait("tensor", S["dve"], marks[kc])
                    em.do("tensor", lambda e, tt=tt, kc=kc, psv=psv, wv_p=wv_p: e.matmul(
                        psv, self.xn.ap()[:, kc * T + tt * 128: kc * T + (tt + 1) * 128],
                        wv_p[:, kc * 128:(kc + 1) * 128],
                        start=(kc == 0), stop=False, skip_group_check=True))
                em.do("tensor", lambda e, tt=tt, psv=psv, hp=hp: e.matmul(
                    psv, self.ones1r[:], self.bvrow[0:1, hp * 128:(hp + 1) * 128],
                    start=False, stop=True, skip_group_check=True))
                em.wait("vector", S["pe"])
                em.do("vector", lambda e, tt=tt, psv=psv: e.tensor_copy(
                    out=self.vp.ap()[:, tt * 128:(tt + 1) * 128], in_=psv))
            em.wait("tensor", S["dve"])

        if not getattr(self, "_l1a_prefetched", False):
            l1a_weights_dma(0, 0)()
        wq0, wk0, wv0, marks0 = l1a_weights_cast(0, 0)
        l1a_weights_dma(1, S["dve"].n)()
        self.emit_ln(use_x32_cast=True, dst=self.chunks(self.xn), mode="norm",
                     src_marks=getattr(self, "_add_marks", None),
                     sq_eng="scalar", sq_scratch=self.chunks(self.xn))
        xnm1 = dict(self._apply_marks)
        proj_inline(0, wq0, qbuf[0], "bq1", marks=xnm1)
        proj_inline(0, wk0, kbuf[0], "bk1", marks=xnm1)
        pe_proj_mark[0] = S["pe"].n
        vprod_inline(0, wv0, marks=xnm1)
        self._vp_gate = 0
        for hp in range(NP):
            if hp + 1 < NP:
                wqn, wkn, wvn, marksn = l1a_weights_cast(hp + 1, pe_proj_mark[(hp + 1) % 2])
                self.vp2 = vbuf[(hp + 1) % 2]
                sides = (proj_sides(hp + 1, wqn, qbuf[(hp + 1) % 2], "bq1", marksn, "wq1")
                         + proj_sides(hp + 1, wkn, kbuf[(hp + 1) % 2], "bk1", marksn, "wk1")
                         + vprod_sides(hp + 1, wvn, marksn))
                if hp + 2 < NP:
                    sides = sides + [l1a_weights_dma(hp + 2, S["dve"].n)]
                self._side = sides
            else:
                wops, self._wo1_pre = self.prestream_sides("wo1", self._l1a_gate)
                self._side = wops
            self._qsrc = qbuf[hp % 2]
            self._ksrc = kbuf[hp % 2]
            self._vsrc = vbuf[hp % 2]
            if hp == 0:
                self.dump_one("qt", self.qtp[:, :])
                self.dump_one("kv", self.kvp[:, :])
                self.dump_one("vp", self.vp.ap()[:, :])
            self._attn_loop(hp, causal=True)
            pe_proj_mark[(hp + 1) % 2] = S["pe"].n
        self._vsrc = None
        self._qsrc = None
        self._ksrc = None
        self.proj_full("wo1", self.outT, self.xn, bias=self.bias["bo1"],
                       prestream=self._wo1_pre)
        self.dump_full("z", self.xn)
        self.emit_ln(srcbf=self.chunks(self.xn), mode="add")

    def emit_l1f(self):
        em, S = self.em, self.S
        self._l1f_gate = (S["act"].n, S["dve"].n, S["pool"].n)
        self.emit_ln(use_x32_cast=True, dst=self.chunks(self.xn), mode="norm",
                     src_marks=getattr(self, "_add_marks", None),
                     sq_eng="scalar", sq_scratch=self.chunks(self.xn))
        tm = dict(self._apply_marks)
        self.dump_full("t", self.xn)
        em.wait("sync", S["dve"])
        em.wait("sync", S["pool"])
        for c in range(CH):
            em.do("sync", lambda e, c=c: e.dma_start(
                out=self.x_spill[c * 128:(c + 1) * 128, :],
                in_=self.xb.ap()[:, c * T:(c + 1) * T]), inc=S["sy"], inc_n=16)
        self.dsig()
        spill_sig = S["syd"].n
        self.emit_ln(srcbf=self.chunks(self.xn), mode="gamma",
                     gamma=self.bias["lnsc"], beta=self.bias["lnbi"], half=0,
                     dst=lambda c: self.pairs.ap()[:, c * 1024:(c + 1) * 1024],
                     src_marks=tm, sq_eng="scalar",
                     sq_scratch=self.chunks(self.outT))
        um = dict(self._apply_marks)
        self.stream_weight("awT", self.warena.ap(), n_chunks=CH, cols=1024,
                           cast_eng="scalar", sync_gate=self._l1f_gate)
        for hf in range(2):
            if hf == 1:
                self.ln_apply(srcbf=self.chunks(self.xn), mode="gamma",
                              gamma=self.bias["lnsc"], beta=self.bias["lnbi"], half=1,
                              dst=lambda c: self.pairs.ap()[:, c * 1024:(c + 1) * 1024])
                um = dict(self._apply_marks)
            em.wait("tensor", S["act"])
            for oc in range(CH):
                ps = self.psA.ap()[:, (oc % 4) * 1024:(oc % 4) * 1024 + 1024]
                if oc >= 4:
                    em.wait("tensor", S["dve"])
                for kc in range(CH):
                    if oc == 0:
                        em.wait("tensor", S["dve"], um[kc])
                    for j in range(2):
                        em.do("tensor", lambda e, oc=oc, kc=kc, j=j, ps=ps: e.matmul(
                            ps[:, j * 512:(j + 1) * 512],
                            self.warena.ap()[:, kc * 1024 + oc * 128: kc * 1024 + (oc + 1) * 128],
                            self.pairs.ap()[:, kc * 1024 + j * 512: kc * 1024 + (j + 1) * 512],
                            start=(kc == 0), stop=(kc == CH - 1), skip_group_check=True),
                            inc=S["pe"] if (kc == CH - 1 and j == 1) else None)
                em.wait("vector", S["pe"])
                em.do("vector", lambda e, oc=oc, ps=ps, hf=hf: e.scalar_tensor_tensor(
                    out=self.outT.ap()[:, oc * T + hf * 1024: oc * T + hf * 1024 + 1024],
                    in0=ps, scalar=self.bias["ab1"][:, oc:oc + 1],
                    in1=self.xn.ap()[:, oc * T + hf * 1024: oc * T + hf * 1024 + 1024],
                    op0=ALU.add, op1=ALU.add), inc=S["dve"])
            em.wait("tensor", S["dve"])
        if "xa" in self.dbg_outs:
            self.dump_full("xa", self.outT)
        sx = "ab"
        def hchunk(hh):
            if hh < 16:
                return self.xb.ap()[:, hh * 1024:(hh + 1) * 1024]
            return self.xn.ap()[:, (hh - 16) * 1024:(hh - 15) * 1024]
        for hf in range(2):
            w1n, w2n = "w1" + sx[hf], "w2" + sx[hf]
            b1n, b2n = "b1" + sx[hf], "b2" + sx[hf]
            w1_pe_marks = {}
            w1_cast_marks = {}
            def w1_stream(hs):
                wsl = self.warena.ap()[:, (hs % 2) * 4096:(hs % 2) * 4096 + 4096]
                self.stream_weight(w1n, wsl, n_chunks=CH, cols=512, col0=hs * 512,
                                   pe_guard=w1_pe_marks.get(hs - 2, None))
                w1_cast_marks[hs] = (S["dve"].n, S["pool"].n)
                return wsl
            wsl_next = w1_stream(0)
            for hs in range(CH):
                wsl = wsl_next
                dn, pn = w1_cast_marks[hs]
                em.wait("tensor", S["dve"], dn)
                em.wait("tensor", S["pool"], pn)
                if hf == 0 and hs == 0:
                    em.wait("scalar", S["syd"], spill_sig)
                if hs + 1 < CH:
                    wsl_next = w1_stream(hs + 1)
                w1_pe_marks[hs] = None
                for hc in range(4):
                    hh = hs * 4 + hc
                    ps = self.psA.ap()[:, (hh % 4) * 1024:(hh % 4) * 1024 + 1024]
                    if hh >= 4:
                        em.wait("tensor", S["act"])
                    for kc in range(CH):
                        for j in range(2):
                            em.do("tensor", lambda e, hc=hc, kc=kc, j=j, ps=ps, hs=hs, wsl=wsl, hf=hf: e.matmul(
                                ps[:, j * 512:(j + 1) * 512],
                                wsl[:, kc * 512 + hc * 128: kc * 512 + (hc + 1) * 128],
                                self.outT.ap()[:, kc * T + hf * 1024 + j * 512: kc * T + hf * 1024 + (j + 1) * 512],
                                start=(kc == 0), stop=(kc == CH - 1), skip_group_check=True),
                                inc=S["pe"] if (kc == CH - 1 and j == 1) else None)
                    em.wait("scalar", S["pe"])
                    em.do("scalar", lambda e, hh=hh, ps=ps, b1n=b1n: e.activation(
                        out=hchunk(hh), in_=ps, func=AF.Gelu,
                        bias=self.bias[b1n][:, hh:hh + 1]))
                w1_pe_marks[hs] = S["pe"].n
            if hf == 0 and "h0" in self.dbg_outs:
                self.dump_one("h0", hchunk(0))
            em.do("scalar", lambda e: e.nop(), inc=S["act"])
            em.wait("tensor", S["act"])
            for ocg in range(2):
                self._w2_entry_pe = S["pe"].n
                em.wait("sync", S["pool"])
                em.wait("sync", S["dve"])
                cast_marks = []
                pe_marks = []
                def w2dma(hp2):
                    st = self.stage[hp2 % 2]
                    if hp2 >= 2:
                        sem, cnt = cast_marks[hp2 - 2]
                        em.wait("sync", sem, cnt)
                    srcw = self.ins[w2n].rearrange("(c p) o -> p c o", p=128)[
                        :, hp2 * 2:hp2 * 2 + 2, ocg * 512:(ocg + 1) * 512]
                    dsem = S["sy"] if hp2 % 2 == 0 else S["sy2"]
                    em.do("sync", lambda e, st=st, srcw=srcw: e.dma_start(
                        out=st[:].rearrange("p (c o) -> p c o", c=2), in_=srcw),
                        inc=dsem, inc_n=16)
                    return dsem
                dsems = [w2dma(0)]
                for hp2 in range(16):
                    if hp2 + 1 < 16:
                        dsems.append(w2dma(hp2 + 1))
                    em.wait("sync", dsems[hp2])
                    em.do("sync", lambda e: e.nop(), inc=S["syd"])
                    ceng = "vector"
                    em.wait(ceng, S["syd"])
                    if hp2 >= 2:
                        em.wait(ceng, S["pe"], pe_marks[hp2 - 2])
                    else:
                        em.wait(ceng, S["pe"], self._w2_entry_pe)
                    st = self.stage[hp2 % 2]
                    w2sl = self.warena.ap()[:, (hp2 % 2) * 4096:(hp2 % 2) * 4096 + 1024]
                    em.do(ceng, lambda e, st=st, w2sl=w2sl: e.tensor_copy(out=w2sl, in_=st[:]))
                    cs = S["dve"] if ceng == "vector" else S["pool"]
                    cast_marks.append((cs, cs.n))
                    em.wait("tensor", cs)
                    for sub in range(2):
                        hk = hp2 * 2 + sub
                        for oc4 in range(4):
                            for tq in range(2):
                                bank = oc4 * 2 + tq
                                em.do("tensor", lambda e, sub=sub, oc4=oc4, tq=tq, bank=bank, hk=hk, w2sl=w2sl, hf=hf: e.matmul(
                                    self.psA.ap()[:, bank * 512:(bank + 1) * 512],
                                    w2sl[:, sub * 512 + oc4 * 128: sub * 512 + (oc4 + 1) * 128],
                                    hchunk(hk)[:, tq * 512:(tq + 1) * 512],
                                    start=(hk == 0), stop=(hk == 31), skip_group_check=True))
                    pe_marks.append(S["pe"].n)
                em.wait("vector", S["pe"])
                for oc4 in range(4):
                    oc = ocg * 4 + oc4
                    for tq in range(2):
                        bank = oc4 * 2 + tq
                        em.do("vector", lambda e, oc=oc, tq=tq, bank=bank, b2n=b2n: e.tensor_scalar_add(
                            out=self.eo_chunk(oc)[:, tq * 512:(tq + 1) * 512],
                            in0=self.psA.ap()[:, bank * 512:(bank + 1) * 512],
                            scalar1=self.bias[b2n][:, oc:oc + 1]),
                            inc=S["dve"] if (oc4 == 3 and tq == 1) else None)
                em.wait("tensor", S["dve"])
            if hf == 0 and "eo" in self.dbg_outs:
                for c in range(CH):
                    self.dump_one("eo", self.eo_chunk(c))
            self.stream_weight("nwT", self.warena.ap(), n_chunks=CH, cols=1024)
            self.wait_w("tensor")
            if hf == 1:
                em.wait("sync", S["pe"])
                em.wait("sync", S["act"])
                for c in range(CH):
                    em.do("sync", lambda e, c=c: e.dma_start(
                        out=self.xb.ap()[:, c * T:(c + 1) * T],
                        in_=self.x_spill[c * 128:(c + 1) * 128, :]), inc=S["sy"], inc_n=16)
                self.dsig()
            for oc in range(CH):
                ps = self.psA.ap()[:, (oc % 4) * 1024:(oc % 4) * 1024 + 1024]
                if oc >= 4:
                    em.wait("tensor", S["act"])
                for kc in range(CH):
                    for j in range(2):
                        em.do("tensor", lambda e, oc=oc, kc=kc, j=j, ps=ps: e.matmul(
                            ps[:, j * 512:(j + 1) * 512],
                            self.warena.ap()[:, kc * 1024 + oc * 128: kc * 1024 + (oc + 1) * 128],
                            self.eo_chunk(kc)[:, j * 512:(j + 1) * 512],
                            start=(kc == 0), stop=(kc == CH - 1), skip_group_check=True),
                            inc=S["pe"] if (kc == CH - 1 and j == 1) else None)
                em.wait("scalar", S["pe"])
                em.do("scalar", lambda e, oc=oc, ps=ps, hf=hf: e.activation(
                    out=self.outT.ap()[:, oc * T + hf * 1024: oc * T + hf * 1024 + 1024],
                    in_=ps, func=AF.Gelu, bias=self.bias["nb"][:, oc:oc + 1]))
            em.do("scalar", lambda e: e.nop(), inc=S["act"])
            em.wait("tensor", S["act"])
        em.wait("vector", S["syd"])
        self.dump_full("y", self.outT)
        self.emit_ln(srcbf=self.chunks(self.outT), mode="add")

    def build(self):
        self.emit_consts()
        if "l0a" in self.blocks:
            self.emit_l0a()
        if "l0f" in self.blocks:
            self.emit_l0f()
        if "l1a" in self.blocks:
            self.emit_l1a()
        if "l1f" in self.blocks:
            self.emit_l1f()
        self.emit_store()
        with self.nc.Block() as block:
            self.em.replay(block)
        self._es.close()
        return self.nc


def rotary_tables():
    inv = 1.0 / 10000.0 ** (np.arange(0, DH, 2, dtype=np.float64) / DH)
    freqs = np.outer(np.arange(T, dtype=np.float64), inv)
    cos = np.cos(freqs).astype(np.float32)
    sin = np.sin(freqs).astype(np.float32)
    cosF = np.ascontiguousarray(np.tile(cos.T, (4, 1))).astype(bfl)
    sinF = np.ascontiguousarray(np.tile(sin.T, (4, 1))).astype(bfl)
    return cosF, sinF

def perm_matrix():
    P = np.zeros((128, 128), dtype=np.float32)
    for blk in range(2):
        b = blk * 64
        for i in range(32):
            P[b + i, b + 32 + i] = -1.0
            P[b + 32 + i, b + i] = 1.0
    return np.ascontiguousarray(P.T.astype(bfl))

def bias_table(b, n=CH):
    return np.ascontiguousarray(np.asarray(b, dtype=np.float32).reshape(n, 128).T)

def cm(a):
    return np.ascontiguousarray(np.asarray(a, dtype=np.float32))

def prep_weights(inputs, blocks):
    d = {}
    if "l0a" in blocks:
        cosF, sinF = rotary_tables()
        d.update(wq0=cm(inputs["L0_attn_Wq"]), wkv0=cm(inputs["L0_attn_Wkv"]),
                 wout0=cm(inputs["L0_attn_Wout"]), bout0=bias_table(inputs["L0_attn_bout"]),
                 cosf=cosF, sinf=sinF, permT=perm_matrix())
    if "l0f" in blocks:
        d.update(w0T=cm(np.asarray(inputs["L0_ffn_W"]).T),
                 rT=cm(np.asarray(inputs["L0_ffn_R"]).T),
                 aT=cm(np.asarray(inputs["L0_ffn_A"]).reshape(128, D).T),
                 bT=cm(np.asarray(inputs["L0_ffn_B"]).transpose(0, 2, 1).reshape(128, D)),
                 b0=bias_table(inputs["L0_ffn_b"]))
    if "l1a" in blocks:
        d.update(wq1=cm(inputs["L1_attn_Wq"]), wk1=cm(inputs["L1_attn_Wk"]),
                 wv1=cm(inputs["L1_attn_Wv"]), wo1=cm(inputs["L1_attn_Wo"]),
                 bq1=bias_table(inputs["L1_attn_bq"]), bk1=bias_table(inputs["L1_attn_bk"]),
                 bo1=bias_table(inputs["L1_attn_bo"]),
                 bv1row=cm(np.asarray(inputs["L1_attn_bv"]).reshape(1, D)))
    if "l1f" in blocks:
        d.update(lnsc=bias_table(inputs["L1_ffn_ln_scale"]), lnbi=bias_table(inputs["L1_ffn_ln_bias"]),
                 awT=cm(np.asarray(inputs["L1_ffn_attn_w"]).T), ab1=bias_table(inputs["L1_ffn_attn_b"]),
                 nwT=cm(np.asarray(inputs["L1_ffn_next_w"]).T), nb=bias_table(inputs["L1_ffn_next_b"]))
    return d

def prep_core_experts(inputs, core):
    eA = (2 * core) % 8
    eB = (2 * core + 1) % 8
    w1 = np.asarray(inputs["L1_ffn_w1"]); w2 = np.asarray(inputs["L1_ffn_w2"])
    b1 = np.asarray(inputs["L1_ffn_b1"]); b2 = np.asarray(inputs["L1_ffn_b2"])
    return dict(w1a=cm(w1[eA]), w1b=cm(w1[eB]), w2a=cm(w2[eA]), w2b=cm(w2[eB]),
                b1a=bias_table(b1[eA], n=32), b1b=bias_table(b1[eB], n=32),
                b2a=bias_table(b2[eA]), b2b=bias_table(b2[eB]))


_NC_CACHE = {}

def _build_nc():
    key = "full"
    if key not in _NC_CACHE:
        mk = MK(blocks=("l0a", "l0f", "l1a", "l1f"), debug=())
        _NC_CACHE[key] = mk.build()
    return _NC_CACHE[key]


def kernel(**inputs):
    nc = _build_nc()
    shared = prep_weights(inputs, ("l0a", "l0f", "l1a", "l1f"))
    x = np.asarray(inputs["L0_attn_x"], dtype=np.float32)
    in_maps = []
    for c in range(8):
        m = dict(shared)
        m.update(prep_core_experts(inputs, c))
        m["x_in"] = np.ascontiguousarray(x[c].T)
        in_maps.append(m)
    res = run_bass_kernel_spmd(nc, in_maps, core_ids=list(range(8)))
    out = np.stack([np.ascontiguousarray(r["x_out"].T) for r in res.results])
    return out.astype(np.float32)


# version 33
# speedup vs baseline: 1.1791x; 1.1791x over previous
import numpy as np
import ml_dtypes
import concourse.bass as bass
import concourse.mybir as mybir
from concourse.bass_utils import run_bass_kernel_spmd

F32 = mybir.dt.float32
BF16 = mybir.dt.bfloat16
AF = mybir.ActivationFunctionType
ALU = mybir.AluOpType
f32 = np.float32
bfl = ml_dtypes.bfloat16

D = 1024
T = 2048
CH = 8
NP = 8
DH = 64
EPS = f32(1e-5)
SHIFT = f32(-32.0)
SCALE8 = f32(0.125)

class Sem:
    def __init__(self, h):
        self.h = h
        self.n = 0

class Em:
    ENGS = ("sync", "tensor", "vector", "scalar", "gpsimd")
    SERIAL = {"vector": "dve", "scalar": "act", "gpsimd": "pool"}

    def __init__(self, S=None):
        self.streams = {e: [] for e in self.ENGS}
        self.S = S

    def do(self, eng, f, inc=None, inc_n=1):
        if eng in self.SERIAL:
            s = self.S[self.SERIAL[eng]]
            assert inc is None or inc is s, "engine ops may only inc their own sem"
            v = s.n
            if v > 0:
                self.streams[eng].append(lambda e, s=s, v=v: e.wait_ge(s.h, v))
            s.n += 1
            self.streams[eng].append(lambda e, f=f, s=s: f(e).then_inc(s.h, 1))
            return
        if eng == "tensor":
            s = self.S["pe"]
            assert inc is None or inc is s
            s.n += 1
            self.streams[eng].append(lambda e, f=f, s=s: f(e).then_inc(s.h, 1))
            return
        if inc is not None:
            inc.n += inc_n
            n, s = inc_n, inc
            self.streams[eng].append(lambda e, f=f: f(e).then_inc(s.h, n))
        else:
            self.streams[eng].append(lambda e, f=f: f(e))

    def wait(self, eng, sem, val=None):
        v = sem.n if val is None else val
        assert v <= sem.n, "waiting for a future count"
        if v > 0:
            self.streams[eng].append(lambda e, s=sem, v=v: e.wait_ge(s.h, v))

    def replay(self, block):
        for eng in self.ENGS:
            ops = self.streams[eng]
            if not ops:
                continue
            def body(e, ops=ops):
                for op in ops:
                    op(e)
            getattr(block, eng)(body)

class MK:
    def __init__(self, blocks, debug=()):
        self.blocks = blocks
        self.debug = set(debug)
        nc = bass.Bass("TRN2", target_bir_lowering=False, dynamic_dma_scratch_size=8192)
        self.nc = nc
        self.dbg_outs = {}

        self.x_in = nc.dram_tensor("x_in", [D, T], F32, kind="ExternalInput").ap()
        self.x_out = nc.dram_tensor("x_out", [D, T], F32, kind="ExternalOutput").ap()
        self.x_spill = nc.dram_tensor("x_spill", [D, T], BF16, kind="Internal").ap()
        self.ins = {}
        def di(name, shape, dtype=F32):
            self.ins[name] = nc.dram_tensor(name, shape, dtype, kind="ExternalInput").ap()
        if "l0a" in blocks:
            di("wq0", [D, D]); di("wkv0", [D, D]); di("wout0", [D, D])
            di("bout0", [128, CH])
            di("cosf", [128, T], BF16); di("sinf", [128, T], BF16)
            di("permT", [128, 128], BF16)
        if "l0f" in blocks:
            di("w0T", [D, D]); di("rT", [D, 8]); di("aT", [D, 128]); di("bT", [128, D])
            di("b0", [128, CH])
        if "l1a" in blocks:
            di("wq1", [D, D]); di("wk1", [D, D]); di("wv1", [D, D]); di("wo1", [D, D])
            di("bq1", [128, CH]); di("bk1", [128, CH]); di("bo1", [128, CH])
            di("bv1row", [1, D])
        if "l1f" in blocks:
            di("lnsc", [128, CH]); di("lnbi", [128, CH])
            di("awT", [D, D]); di("ab1", [128, CH])
            di("w1a", [D, 4096]); di("w1b", [D, 4096])
            di("w2a", [4096, D]); di("w2b", [4096, D])
            di("b1a", [128, 32]); di("b1b", [128, 32])
            di("b2a", [128, CH]); di("b2b", [128, CH])
            di("nwT", [D, D]); di("nb", [128, CH])

        sb = nc.alloc_sbuf_tensor
        self.xb = sb("xb", [128, CH * T], BF16)
        self.xn = sb("xn", [128, CH * T], BF16)
        self.outT = sb("outTb", [128, CH * T], BF16)
        self.pairs = sb("pairs", [128, 8192], BF16)
        self.qtp = self.pairs.ap()[:, 0:T]
        self.kvp = self.pairs.ap()[:, T:2 * T]
        self.pt = [self.pairs.ap()[:, 2 * T + i * T: 2 * T + (i + 1) * T] for i in range(2)]
        self.ubuf = self.pairs
        self.vp = sb("vp", [128, 2 * T], BF16, is_transpose_dest=True)
        self.warena = sb("warena", [128, 8192], BF16)
        self.stage = [sb(f"stage{i}", [128, 1024], F32) for i in range(2)]
        self.scr16 = sb("scr16", [128, 6144], BF16)
        self.lncast = self.scr16.ap()[:, 0:T]
        self.lnsq = self.scr16.ap()[:, T:2 * T]
        self.aptmp = [self.scr16.ap()[:, (1 + i) * T:(2 + i) * T] for i in range(2)]
        self.stage2nd = [self.scr16.ap()[:, 0:T].bitcast(F32),
                         self.scr16.ap()[:, T:2 * T].bitcast(F32)]
        self.mB = sb("mB", [128, T], BF16)
        self.rB = sb("rB", [128, T], BF16)
        self.rowb = sb("rowb", [128, T], BF16)
        self.rowb2 = sb("rowb2", [128, 1024], BF16)
        self.rotc = sb("rotc", [128, 2 * T + 128], BF16) if "l0a" in blocks else None
        self.ident = sb("ident", [128, 128], BF16)
        self.tril = sb("tril", [128, 128], BF16)
        self.onesD = sb("onesD", [128, 1], BF16)
        self.ones1r = sb("ones1r", [1, 128], BF16)
        self.onespad = sb("onespad", [128, 32], BF16)
        self.selER = sb("selER", [8, 128], BF16)
        self.e33 = sb("e33", [33, 128], BF16)
        self.bias = {}
        for name in ("bout0", "b0", "bq1", "bk1", "bo1", "lnsc", "lnbi", "ab1",
                     "b1a", "b1b", "b2a", "b2b", "nb"):
            if name in self.ins:
                self.bias[name] = sb("sb_" + name, list(self.ins[name].shape), F32)
        if "l1a" in blocks:
            self.bvrow = sb("sb_bvrow", [1, D], BF16)
        if "l0f" in blocks:
            self.gl_tok = None
            self.gscr = None
            self.gscr2 = None
            self.gsml = None
            self.gate_bf = sb("gate_bf", [128, 128], BF16)

        self.psA = nc.alloc_psum_tensor("psA", [128, 4096], F32)

        import contextlib
        self._es = contextlib.ExitStack()
        self.S = {}
        for name in ("pe", "act", "dve", "pool", "sy", "sy2", "syd"):
            self.S[name] = Sem(self._es.enter_context(nc.semaphore(name)))
        self.em = Em(self.S)

        for name in self.debug:
            shp = {"xn": [D, T], "z": [D, T], "outT": [D, T],
                   "qt": [128, T], "kv": [128, T], "vp": [128, T],
                   "pt": [128, T], "mrow": [1, T], "rrow": [1, T],
                   "u": [D, T], "gate": [128, 128], "h0": [128, T],
                   "gh": [128, T], "y": [D, T], "t": [D, T], "xa": [D, T],
                   "eo": [D, T]}[name]
            self.dbg_outs[name] = nc.dram_tensor("dbg_" + name, shp, F32,
                                                 kind="ExternalOutput").ap()

    def eo_chunk(self, c):
        if c < 6:
            return self.scr16.ap()[:, c * 1024:(c + 1) * 1024]
        return (self.mB if c == 6 else self.rB).ap()[:, 0:1024]

    def chunks(self, buf, cols=T):
        return lambda c: buf.ap()[:, c * cols:(c + 1) * cols]

    def wait_w(self, eng):
        self.em.wait(eng, self.S["pool"])
        self.em.wait(eng, self.S["dve"])

    def dsig(self):
        self.em.wait("sync", self.S["sy"])
        self.em.do("sync", lambda e: e.nop(), inc=self.S["syd"])
        return self.S["syd"]

    def dump_full(self, name, buf, n_chunks=CH, cols=T):
        if name not in self.dbg_outs:
            return
        em, S = self.em, self.S
        out = self.dbg_outs[name]
        for c in range(n_chunks):
            w = min(1024, cols)
            for j0 in range(0, cols, w):
                em.do("scalar", lambda e, c=c, j0=j0, w=w: e.copy(
                    out=self.stage[0][:, 0:w],
                    in_=buf.ap()[:, c * cols + j0: c * cols + j0 + w]), inc=S["act"])
                em.wait("sync", S["act"])
                em.do("sync", lambda e, c=c, j0=j0, w=w: e.dma_start(
                    out=out[c * 128:(c + 1) * 128, j0:j0 + w],
                    in_=self.stage[0][:, 0:w]), inc=S["sy"], inc_n=16)
                self.dsig()
                em.wait("scalar", S["syd"])

    def dump_one(self, name, ap, rows=128):
        if name not in self.dbg_outs:
            return
        em, S = self.em, self.S
        out = self.dbg_outs[name]
        cols = ap.shape[-1]
        em.wait("scalar", S["dve"])
        em.wait("scalar", S["pool"])
        for j0 in range(0, cols, 1024):
            w = min(1024, cols - j0)
            em.do("scalar", lambda e, j0=j0, w=w: e.copy(
                out=self.stage[0][0:rows, 0:w], in_=ap[:, j0:j0 + w]), inc=S["act"])
            em.wait("sync", S["act"])
            em.do("sync", lambda e, j0=j0, w=w: e.dma_start(
                out=out[0:rows, j0:j0 + w], in_=self.stage[0][0:rows, 0:w]),
                inc=S["sy"], inc_n=16)
            self.dsig()
            em.wait("scalar", S["syd"])

    def emit_consts(self):
        em, S = self.em, self.S
        g = "gpsimd"
        def pdo(f):
            em.do(g, f)
        pdo(lambda e: e.memset(self.ident[:], 1.0))
        pdo(lambda e: e.affine_select(out=self.ident[:], in_=self.ident[:],
              pattern=[[1, 128]], compare_op=ALU.is_equal, fill=0.0, base=0, channel_multiplier=-1))
        pdo(lambda e: e.memset(self.tril[:], 1.0))
        pdo(lambda e: e.affine_select(out=self.tril[:], in_=self.tril[:],
              pattern=[[1, 128]], compare_op=ALU.is_ge, fill=0.0, base=0, channel_multiplier=-1))
        pdo(lambda e: e.memset(self.onesD[:], 1.0 / 1024.0))
        pdo(lambda e: e.memset(self.ones1r[:], 1.0))
        pdo(lambda e: e.memset(self.onespad[:], 1.0))
        pdo(lambda e: e.memset(self.selER[:], 1.0))
        pdo(lambda e: e.affine_select(out=self.selER[:], in_=self.selER[:],
              pattern=[[1, 128]], compare_op=ALU.is_ge, fill=0.0, base=0, channel_multiplier=-16))
        pdo(lambda e: e.affine_select(out=self.selER[:], in_=self.selER[:],
              pattern=[[-1, 128]], compare_op=ALU.is_ge, fill=0.0, base=15, channel_multiplier=16))
        pdo(lambda e: e.memset(self.e33[:], 0.0))
        pdo(lambda e: e.memset(self.e33[0:1, 0:64], 1.0))
        pdo(lambda e: e.memset(self.e33[32:33, 64:128], 1.0))
        pdo(lambda e: e.memset(self.rowb2[:], 0.0))
        xact_marks = {}
        prev_dsem = None
        for c in range(CH):
            dsem = S["sy"] if c % 2 == 0 else S["sy2"]
            sth = self.stage[c % 2]
            if c >= 2:
                em.wait("sync", S["act"], xact_marks[c - 2])
            em.do("sync", lambda e, c=c, sth=sth: e.dma_start(
                out=sth[:], in_=self.x_in[c * 128:(c + 1) * 128, 0:1024]), inc=dsem, inc_n=16)
            em.do("sync", lambda e, c=c, sth=sth: e.dma_start(
                out=self.stage2nd[c % 2][:], in_=self.x_in[c * 128:(c + 1) * 128, 1024:2048]),
                inc=dsem, inc_n=16)
            em.wait("sync", dsem)
            em.do("sync", lambda e: e.nop(), inc=S["syd"])
            em.wait("scalar", S["syd"])
            em.do("scalar", lambda e, c=c, sth=sth: e.copy(
                out=self.xb.ap()[:, c * T: c * T + 1024], in_=sth[:]))
            em.do("scalar", lambda e, c=c: e.copy(
                out=self.xb.ap()[:, c * T + 1024: c * T + 2048], in_=self.stage2nd[c % 2][:]))
            xact_marks[c] = S["act"].n
        for name, sbuf in self.bias.items():
            em.do("sync", lambda e, n=name, s=sbuf: e.dma_start(out=s[:], in_=self.ins[n]),
                  inc=S["sy"], inc_n=16)
        if "l1a" in self.blocks:
            em.wait("sync", S["act"])
            em.do("sync", lambda e: e.dma_start(out=self.stage[0][0:1, :], in_=self.ins["bv1row"]),
                  inc=S["sy"], inc_n=16)
        self.dsig()
        if "l1a" in self.blocks:
            em.wait("vector", S["syd"])
            em.do("vector", lambda e: e.tensor_copy(out=self.bvrow[:], in_=self.stage[0][0:1, :]),
                  inc=S["dve"])
        self._stage_gate = (S["act"].n, S["dve"].n, S["pool"].n)
        em.wait("vector", S["syd"])
        em.wait("vector", S["pool"])
        em.wait("tensor", S["pool"])
        em.wait("gpsimd", S["syd"])
        em.wait("scalar", S["syd"])

    def emit_ln(self, srcbf=None, use_x32_cast=False, dst=None, mode="norm",
               gamma=None, beta=None, half=None, src_marks=None, sq_eng=None,
               sq_scratch=None):
        em, S = self.em, self.S
        em.wait("gpsimd", S["act"]); em.wait("gpsimd", S["dve"])
        em.wait("vector", S["act"]); em.wait("vector", S["pool"])
        if sq_eng != "scalar":
            em.wait("scalar", S["dve"])
        em.wait("scalar", S["pool"])
        g = getattr(self, "_psA_guard", None)
        if g is None:
            em.wait("tensor", S["act"]); em.wait("tensor", S["dve"])
        else:
            em.wait("tensor", S["act"], g[0]); em.wait("tensor", S["dve"], g[1])
        self._psA_guard = None
        st1 = self.psA.ap()[0:1, 0:2048]
        st2 = self.psA.ap()[32:33, 2048:4096]
        sqb = [self.lnsq, self.mB.ap()[:, 0:T]]
        pe_marks = []
        for c in range(CH):
            if use_x32_cast:
                src_c = self.xb.ap()[:, c * T:(c + 1) * T]
            else:
                src_c = srcbf(c)
            if sq_eng == "scalar":
                sq = sq_scratch(c)
                if src_marks is not None and c in src_marks:
                    em.wait("scalar", S["dve"], src_marks[c])
                em.do("scalar", lambda e, s=src_c, sq=sq: e.activation(
                    out=sq, in_=s, func=AF.Square))
                sqm = S["act"].n
                if src_marks is not None and c in src_marks:
                    em.wait("tensor", S["dve"], src_marks[c])
                em.wait("tensor", S["act"], sqm)
            else:
                sq = sqb[c % 2]
                if c >= 2:
                    em.wait("vector", S["pe"], pe_marks[c - 2])
                em.do("vector", lambda e, s=src_c, sq=sq: e.tensor_tensor(
                    out=sq, in0=s, in1=s, op=ALU.mult))
                em.wait("tensor", S["dve"])
            for j in range(4):
                em.do("tensor", lambda e, c=c, j=j, s=src_c: e.matmul(
                    st1[:, j * 512:(j + 1) * 512], self.onesD[:],
                    s[:, j * 512:(j + 1) * 512], start=(c == 0), stop=(c == CH - 1),
                    skip_group_check=True))
            for j in range(4):
                em.do("tensor", lambda e, c=c, j=j, sq=sq: e.matmul(
                    st2[:, j * 512:(j + 1) * 512], self.onesD[:],
                    sq[:, j * 512:(j + 1) * 512], start=(c == 0), stop=(c == CH - 1),
                    skip_group_check=True))
            pe_marks.append(S["pe"].n)
        em.wait("vector", S["pe"])
        em.do("vector", lambda e: e.tensor_copy(out=self.rowb[0:1, :], in_=st1))
        em.do("vector", lambda e: e.tensor_tensor(out=self.rowb[64:65, :],
              in0=self.rowb[0:1, :], in1=self.rowb[0:1, :], op=ALU.mult))
        em.do("vector", lambda e: e.scalar_tensor_tensor(out=self.rowb[96:97, :],
              in0=st2, scalar=float(EPS), in1=self.rowb[64:65, :],
              op0=ALU.add, op1=ALU.subtract))
        self._veps_dve = S["dve"].n
        em.wait("scalar", S["dve"])
        em.do("scalar", lambda e: e.activation(out=self.rowb[64:65, :], in_=self.rowb[96:97, :], func=AF.Ln))
        em.do("scalar", lambda e: e.activation(out=self.rowb[32:33, :], in_=self.rowb[64:65, :],
              func=AF.Exp, scale=f32(-0.5)))
        em.wait("vector", S["act"])
        em.do("vector", lambda e: e.tensor_copy(out=self.lnsq[0:1, :], in_=self.rowb[32:33, :]))
        bcast = self.psA.ap()[:, 2048:4096]
        em.wait("tensor", S["dve"])
        for j in range(4):
            em.do("tensor", lambda e, j=j: e.matmul(bcast[:, j * 512:(j + 1) * 512],
                  self.ones1r[:], self.rowb[0:1, j * 512:(j + 1) * 512],
                  start=True, stop=True))
        em.wait("scalar", S["pe"])
        em.do("scalar", lambda e: e.copy(out=self.mB[:], in_=bcast))
        em.wait("tensor", S["act"])
        em.wait("tensor", S["dve"])
        for j in range(4):
            em.do("tensor", lambda e, j=j: e.matmul(bcast[:, j * 512:(j + 1) * 512],
                  self.ones1r[:], self.lnsq[0:1, j * 512:(j + 1) * 512],
                  start=True, stop=True))
        em.wait("scalar", S["pe"])
        em.do("scalar", lambda e: e.copy(out=self.rB[:], in_=bcast))
        self._ln_guard = (S["act"].n, self._veps_dve)
        em.wait("vector", S["act"])
        self.ln_apply(srcbf=srcbf, use_x32=use_x32_cast, dst=dst, mode=mode,
                      gamma=gamma, beta=beta, half=half)
        if mode == "add":
            self._psA_guard = self._ln_guard

    def ln_apply(self, srcbf=None, use_x32=False, dst=None, mode="norm",
                 gamma=None, beta=None, half=None):
        em, S = self.em, self.S
        self._add_marks = {}
        self._apply_marks = {}
        lo = 0 if half is None else half * 1024
        w = T if half is None else 1024
        em.wait("vector", S["act"])
        for c in range(CH):
            src_c = (self.xb.ap()[:, c * T + lo: c * T + lo + w] if use_x32
                     else srcbf(c)[:, lo:lo + w])
            eng = "vector"
            t0 = self.aptmp[c % 2][:, 0:w]
            em.do(eng, lambda e, s=src_c, t0=t0, lo=lo, w=w: e.tensor_tensor(
                out=t0, in0=s, in1=self.mB[:, lo:lo + w], op=ALU.subtract))
            if mode == "norm":
                em.do(eng, lambda e, c=c, t0=t0, lo=lo, w=w: e.tensor_tensor(
                    out=dst(c), in0=t0, in1=self.rB[:, lo:lo + w], op=ALU.mult))
            elif mode == "gamma":
                em.do(eng, lambda e, t0=t0, lo=lo, w=w: e.tensor_tensor(
                    out=t0, in0=t0, in1=self.rB[:, lo:lo + w], op=ALU.mult))
                em.do(eng, lambda e, c=c, t0=t0: e.tensor_scalar(
                    out=dst(c), in0=t0, scalar1=gamma[:, c:c + 1], scalar2=beta[:, c:c + 1],
                    op0=ALU.mult, op1=ALU.add))
            else:
                em.do(eng, lambda e, t0=t0, lo=lo, w=w: e.tensor_tensor(
                    out=t0, in0=t0, in1=self.rB[:, lo:lo + w], op=ALU.mult))
                x32c = self.xb.ap()[:, c * T + lo: c * T + lo + w]
                em.do(eng, lambda e, x=x32c, t0=t0: e.tensor_tensor(
                    out=x, in0=x, in1=t0, op=ALU.add))
                self._add_marks[c] = S["dve"].n
            self._apply_marks[c] = S["dve"].n
        em.wait("vector", S["pool"])
        em.wait("gpsimd", S["dve"])

    def prestream_sides(self, wname, sync_gate):
        em, S = self.em, self.S
        pe_guard = S["pe"].n
        state = {"marks": []}
        def mk_dma(i):
            def emit():
                if i < 2:
                    em.wait("sync", S["act"], sync_gate[0])
                    em.wait("sync", S["dve"], sync_gate[1])
                    em.wait("sync", S["pool"], sync_gate[2])
                else:
                    em.wait("sync", S["dve"], state["marks"][i - 2])
                st = self.stage[i % 2]
                src3 = self.ins[wname].rearrange("(c p) o -> p c o", p=128)[:, i:i + 1, :]
                em.do("sync", lambda e, st=st, src3=src3: e.dma_start(
                    out=st[:].rearrange("p (c o) -> p c o", c=1)[:, 0:1, :], in_=src3),
                    inc=S["sy"], inc_n=16)
                self.dsig()
            return emit
        def mk_cast(i):
            def emit():
                em.wait("vector", S["syd"])
                em.wait("vector", S["pe"], pe_guard)
                st = self.stage[i % 2]
                em.do("vector", lambda e, st=st, i=i: e.tensor_copy(
                    out=self.warena.ap()[:, i * 1024:(i + 1) * 1024], in_=st[:, 0:1024]))
                state["marks"].append(S["dve"].n)
                state["mark"] = S["dve"].n
            return emit
        pad = lambda: None
        slots = {0: mk_dma(0), 1: mk_dma(1)}
        for i in range(8):
            slots[7 * (i + 1)] = mk_cast(i)
            if i + 2 < 8:
                slots[7 * (i + 1) + 1] = mk_dma(i + 2)
        n = max(slots) + 1
        return [slots.get(k, pad) for k in range(n)], state

    def stream_weight(self, src_name, dst_bf_ap, n_chunks=CH, cols=1024, col0=0,
                      dst_off=0, rearr=True, pe_guard=None, cast_eng=None,
                      sync_gate=None, stage_off=0, deep_stage=False):
        em, S = self.em, self.S
        src = self.ins[src_name]
        if sync_gate is None:
            em.wait("sync", S["pool"])
            em.wait("sync", S["dve"])
        else:
            em.wait("sync", S["act"], sync_gate[0])
            em.wait("sync", S["dve"], sync_gate[1])
            em.wait("sync", S["pool"], sync_gate[2])
        ceng = cast_eng or "vector"
        if pe_guard is None:
            em.wait("gpsimd", S["pe"])
            em.wait("vector", S["pe"])
            if ceng == "scalar":
                em.wait("scalar", S["pe"])
        elif pe_guard > 0:
            em.wait("gpsimd", S["pe"], pe_guard)
            em.wait("vector", S["pe"], pe_guard)
            if ceng == "scalar":
                em.wait("scalar", S["pe"], pe_guard)
        cast_sem = {"vector": S["dve"], "scalar": S["act"]}.get(ceng, S["pool"])
        nch_per = max(1, 1024 // cols)
        i = 0
        it = 0
        cast_marks = []
        sbufs = self.stage if not deep_stage else self.stage + self.stage2nd
        nst = len(sbufs)
        while i < n_chunks:
            k = min(nch_per, n_chunks - i)
            st = sbufs[(it + stage_off) % nst]
            if it >= nst:
                sem, cnt = cast_marks[it - nst]
                em.wait("sync", sem, cnt)
            src3 = src.rearrange("(c p) o -> p c o", p=128)[:, i:i + k, col0:col0 + cols]
            em.do("sync", lambda e, st=st, src3=src3, k=k, cols=cols: e.dma_start(
                out=st[:].rearrange("p (c o) -> p c o", c=1024 // cols)[:, 0:k, :], in_=src3),
                inc=S["sy"], inc_n=16)
            self.dsig()
            em.wait(ceng, S["syd"])
            if ceng == "scalar":
                em.do(ceng, lambda e, st=st, i=i, k=k, cols=cols: e.copy(
                    out=dst_bf_ap[:, dst_off + i * cols: dst_off + (i + k) * cols],
                    in_=st[:, 0:k * cols]))
            else:
                em.do(ceng, lambda e, st=st, i=i, k=k, cols=cols: e.tensor_copy(
                    out=dst_bf_ap[:, dst_off + i * cols: dst_off + (i + k) * cols],
                    in_=st[:, 0:k * cols]))
            cast_marks.append((cast_sem, cast_sem.n))
            i += k
            it += 1

    def proj_full(self, wname, src_bf, dst_bf, bias=None, gelu=False,
                  add_from=None, src_cols=T, src_lo=0, width=None, dst_cols=None,
                  dst_lo=0, wT_dram=None, sync_gate=None, prestream=None):
        em, S = self.em, self.S
        w = width or src_cols
        dst_cols = dst_cols or src_cols
        if prestream is not None:
            em.wait("tensor", S["dve"], prestream["mark"])
        else:
            self.stream_weight(wname, self.warena.ap(), n_chunks=CH, cols=1024,
                               sync_gate=sync_gate, cast_eng="scalar")
            self.wait_w("tensor")
        em.wait("tensor", S["act"])
        em.wait("tensor", S["dve"])
        nj = (w + 511) // 512
        for oc in range(CH):
            ps = self.psA.ap()[:, (oc % 2) * 2048:(oc % 2) * 2048 + w]
            if oc >= 2:
                em.wait("tensor", S["act"])
                em.wait("tensor", S["dve"])
            for kc in range(CH):
                for j in range(nj):
                    j0, j1 = j * 512, min((j + 1) * 512, w)
                    em.do("tensor", lambda e, oc=oc, kc=kc, j0=j0, j1=j1, ps=ps: e.matmul(
                        ps[:, j0:j1],
                        self.warena.ap()[:, kc * 1024 + oc * 128: kc * 1024 + (oc + 1) * 128],
                        src_bf.ap()[:, kc * src_cols + src_lo + j0: kc * src_cols + src_lo + j1],
                        start=(kc == 0), stop=(kc == CH - 1), skip_group_check=True),
                        inc=S["pe"] if (kc == CH - 1 and j == nj - 1) else None)
            dst_ap = dst_bf.ap()[:, oc * dst_cols + dst_lo: oc * dst_cols + dst_lo + w]
            if gelu:
                em.wait("scalar", S["pe"])
                b = bias[:, oc:oc + 1] if bias is not None else f32(0.0)
                em.do("scalar", lambda e, ps=ps, b=b, d=dst_ap: e.activation(
                    out=d, in_=ps, func=AF.Gelu, bias=b), inc=S["act"])
            elif add_from is not None:
                em.wait("vector", S["pe"])
                b = bias[:, oc:oc + 1] if bias is not None else 0.0
                em.do("vector", lambda e, ps=ps, b=b, d=dst_ap, oc=oc: e.scalar_tensor_tensor(
                    out=d, in0=ps, scalar=b, in1=add_from(oc),
                    op0=ALU.add, op1=ALU.add), inc=S["dve"])
            elif bias is not None:
                em.wait("vector", S["pe"])
                em.do("vector", lambda e, ps=ps, oc=oc, d=dst_ap: e.tensor_scalar_add(
                    out=d, in0=ps, scalar1=bias[:, oc:oc + 1]), inc=S["dve"])
            else:
                em.wait("scalar", S["pe"])
                em.do("scalar", lambda e, ps=ps, d=dst_ap: e.copy(out=d, in_=ps), inc=S["act"])
        em.wait("tensor", S["dve"])
        em.wait("tensor", S["act"])

    def emit_l0a(self):
        em, S = self.em, self.S
        cosf = self.rotc.ap()[:, 0:T]
        sinf = self.rotc.ap()[:, T:2 * T]
        permT = self.rotc.ap()[:, 2 * T:2 * T + 128]
        em.do("sync", lambda e: e.dma_start(out=cosf, in_=self.ins["cosf"]), inc=S["sy"], inc_n=16)
        em.do("sync", lambda e: e.dma_start(out=sinf, in_=self.ins["sinf"]), inc=S["sy"], inc_n=16)
        em.do("sync", lambda e: e.dma_start(out=permT, in_=self.ins["permT"]), inc=S["sy"], inc_n=16)
        self.dsig()
        consts_ld = S["syd"].n

        qbuf = [self.qtp, self.scr16.ap()[:, T:2 * T]]
        kbuf = [self.kvp, self.scr16.ap()[:, 2 * T:3 * T]]
        vpar = [self.vp.ap()[:, 0:T], self.vp.ap()[:, T:2 * T]]
        pe_proj_mark = [0, 0]
        if not hasattr(self, "_side_bank_mark"):
            self._side_bank_mark = {0: 0, 1: 0}

        wf32 = self.warena.ap().bitcast(F32)

        def l0a_weights_dma(hp, dve_gate):
            def emit():
                em.wait("sync", S["dve"], dve_gate)
                for (wname, scr) in (("wq0", wf32[:, 1024:2048]), ("wkv0", wf32[:, 3072:4096])):
                    src3 = self.ins[wname].rearrange("(c p) o -> p c o", p=128)[:, :, hp * 128:(hp + 1) * 128]
                    em.do("sync", lambda e, scr=scr, src3=src3: e.dma_start(
                        out=scr.rearrange("p (c o) -> p c o", c=8), in_=src3), inc=S["sy"], inc_n=16)
                    self.dsig()
            return emit

        def l0a_weights_cast(hp, pe_guard):
            slot = (hp % 2) * 4096
            wq_p = self.warena.ap()[:, slot:slot + 1024]
            wkv_p = self.warena.ap()[:, slot + 1024:slot + 2048]
            marks = {}
            em.wait("vector", S["syd"])
            em.wait("vector", S["pe"], pe_guard)
            for (wname, scr, dstw) in (("wq0", wf32[:, 1024:2048], wq_p),
                                       ("wkv0", wf32[:, 3072:4096], wkv_p)):
                em.do("vector", lambda e, scr=scr, d=dstw: e.tensor_copy(out=d, in_=scr))
                marks[wname] = (S["dve"].n, S["pool"].n)
            return wq_p, wkv_p, marks

        def proj_sides(wsrc, dstp, marks, wname):
            ops = []
            for j in range(4):
                pb = self.psA.ap()[:, 3072 + (j % 2) * 512: 3072 + (j % 2) * 512 + 512]
                def mk_mm(kc0, j=j, pb=pb, first=(j == 0)):
                    def emit():
                        if first and kc0 == 0:
                            dn, pn = marks[wname]
                            em.wait("tensor", S["dve"], dn)
                            em.wait("tensor", S["pool"], pn)
                        if kc0 == 0:
                            em.wait("tensor", S["dve"], self._side_bank_mark[j % 2])
                        for kc in (kc0, kc0 + 1):
                            em.do("tensor", lambda e, kc=kc, j=j, pb=pb: e.matmul(
                                pb, wsrc[:, kc * 128:(kc + 1) * 128],
                                self.xn.ap()[:, kc * T + j * 512: kc * T + (j + 1) * 512],
                                start=(kc == 0), stop=(kc == CH - 1), skip_group_check=True))
                    return emit
                for kc0 in (0, 2, 4, 6):
                    ops.append(mk_mm(kc0))
                def mk_evac(j=j, pb=pb):
                    def emit():
                        em.wait("vector", S["pe"])
                        em.do("vector", lambda e, j=j, pb=pb: e.tensor_copy(
                            out=dstp[:, j * 512:(j + 1) * 512], in_=pb))
                        self._side_bank_mark[j % 2] = S["dve"].n
                    return emit
                ops.append(mk_evac())
            return ops

        def rot_sides(buf):
            ops = []
            for j in range(4):
                pb = self.psA.ap()[:, 3072 + (j % 2) * 512: 3072 + (j % 2) * 512 + 512]
                sl = slice(j * 512, (j + 1) * 512)
                rot_sl = self.mB.ap()[:, sl]
                t2_sl = self.rB.ap()[:, sl]
                def mk_rotmm(j=j, pb=pb, sl=sl):
                    def emit():
                        em.wait("tensor", S["dve"], self._side_bank_mark[j % 2])
                        em.do("tensor", lambda e, pb=pb, sl=sl: e.matmul(
                            pb, permT, buf[:, sl], start=True, stop=True))
                    return emit
                ops.append(mk_rotmm())
                def mk_evac(j=j, pb=pb, rot_sl=rot_sl):
                    def emit():
                        em.wait("vector", S["pe"])
                        em.do("vector", lambda e, pb=pb, rot_sl=rot_sl: e.tensor_copy(
                            out=rot_sl, in_=pb))
                        self._side_bank_mark[j % 2] = S["dve"].n
                    return emit
                ops.append(mk_evac())
                def mk_tt(j=j, sl=sl, rot_sl=rot_sl, t2_sl=t2_sl):
                    def emit():
                        em.wait("vector", S["pe"])
                        em.do("vector", lambda e, sl=sl, t2_sl=t2_sl: e.tensor_tensor(
                            out=t2_sl, in0=buf[:, sl], in1=cosf[:, sl], op=ALU.mult))
                        em.do("vector", lambda e, sl=sl, rot_sl=rot_sl: e.tensor_tensor(
                            out=rot_sl, in0=rot_sl, in1=sinf[:, sl], op=ALU.mult))
                        em.do("vector", lambda e, sl=sl, rot_sl=rot_sl, t2_sl=t2_sl: e.tensor_tensor(
                            out=buf[:, sl], in0=rot_sl, in1=t2_sl, op=ALU.add))
                    return emit
                ops.append(mk_tt())
            return ops

        def proj_inline(wsrc, dstp, marks=None):
            em.wait("tensor", S["pool"])
            if marks is None:
                em.wait("tensor", S["dve"])
            else:
                em.wait("tensor", S["dve"], getattr(self, "_ps_evac_mark", 0))
            em.wait("tensor", S["act"])
            ps = self.psA.ap()[:, 0:2048]
            for kc in range(CH):
                if marks is not None:
                    em.wait("tensor", S["dve"], marks[kc])
                for j in range(4):
                    em.do("tensor", lambda e, kc=kc, j=j, w=wsrc, ps=ps: e.matmul(
                        ps[:, j * 512:(j + 1) * 512], w[:, kc * 128:(kc + 1) * 128],
                        self.xn.ap()[:, kc * T + j * 512: kc * T + (j + 1) * 512],
                        start=(kc == 0), stop=(kc == CH - 1), skip_group_check=True))
            em.wait("vector", S["pe"])
            em.do("vector", lambda e, d=dstp, ps=ps: e.tensor_copy(out=d[:, :], in_=ps))
            self._ps_evac_mark = S["dve"].n

        def rotary_inline(buf):
            rp = self.psA.ap()[:, 2048:4096]
            em.wait("tensor", S["dve"])
            em.wait("tensor", S["act"])
            for j in range(4):
                em.do("tensor", lambda e, j=j, buf=buf, rp=rp: e.matmul(
                    rp[:, j * 512:(j + 1) * 512], permT, buf[:, j * 512:(j + 1) * 512],
                    start=True, stop=True))
            em.wait("scalar", S["pe"])
            rot_bf = self.pt[0]
            em.do("scalar", lambda e, r=rot_bf, rp=rp: e.copy(out=r[:, :], in_=rp))
            t2 = self.pt[1]
            em.wait("vector", S["pe"])
            em.wait("vector", S["act"])
            em.do("vector", lambda e, buf=buf, t2=t2: e.tensor_tensor(
                out=t2[:, :], in0=buf, in1=cosf, op=ALU.mult))
            em.do("vector", lambda e, r=rot_bf: e.tensor_tensor(
                out=r[:, :], in0=r[:, :], in1=sinf, op=ALU.mult))
            em.do("vector", lambda e, r=rot_bf, t2=t2, buf=buf: e.tensor_tensor(
                out=buf, in0=r[:, :], in1=t2[:, :], op=ALU.add))

        def vtrans_side(buf, dst):
            def emit():
                em.wait("sync", S["dve"])
                em.wait("sync", S["pe"])
                for tt in range(16):
                    em.do("sync", lambda e, tt=tt, buf=buf, dst=dst: e.dma_start_transpose(
                        out=dst[:, tt * 128:(tt + 1) * 128],
                        in_=buf[:, tt * 128:(tt + 1) * 128]), inc=S["sy"], inc_n=16)
                self.dsig()
                self._vp_gate = S["syd"].n
            return emit

        def vtrans(buf, dst):
            em.wait("sync", S["dve"])
            em.wait("sync", S["pe"])
            for tt in range(16):
                em.do("sync", lambda e, tt=tt, buf=buf, dst=dst: e.dma_start_transpose(
                    out=dst[:, tt * 128:(tt + 1) * 128],
                    in_=buf[:, tt * 128:(tt + 1) * 128]), inc=S["sy"], inc_n=16)
            self.dsig()
            self._vp_gate = S["syd"].n

        l0a_weights_dma(0, 0)()
        wq0, wkv0, marks0 = l0a_weights_cast(0, 0)
        l0a_weights_dma(1, S["dve"].n)()
        self.emit_ln(use_x32_cast=True, dst=self.chunks(self.xn), mode="norm",
                     src_marks=getattr(self, "_add_marks", None))
        self.dump_full("xn", self.xn)
        xnm0 = dict(self._apply_marks)
        em.wait("gpsimd", S["syd"], consts_ld)
        em.wait("vector", S["syd"], consts_ld)
        proj_inline(wq0, qbuf[0], marks=xnm0)
        rotary_inline(qbuf[0][:, :])
        proj_inline(wkv0, kbuf[0], marks=xnm0)
        rotary_inline(kbuf[0][:, :])
        pe_proj_mark[0] = S["pe"].n
        vtrans(kbuf[0][:, :], vpar[0])
        rot_mark = S["dve"].n
        for hp in range(NP):
            if hp + 1 < NP:
                wqn, wkvn, marksn = l0a_weights_cast(hp + 1, pe_proj_mark[(hp + 1) % 2])
                self._side = (proj_sides(wqn, qbuf[(hp + 1) % 2], marksn, "wq0")
                              + rot_sides(qbuf[(hp + 1) % 2])
                              + proj_sides(wkvn, kbuf[(hp + 1) % 2], marksn, "wkv0")
                              + rot_sides(kbuf[(hp + 1) % 2])
                              + [vtrans_side(kbuf[(hp + 1) % 2][:, :], vpar[(hp + 1) % 2])])
                if hp + 2 < NP:
                    self._side = self._side + [l0a_weights_dma(hp + 2, S["dve"].n)]
            else:
                wops, self._wout0_pre = self.prestream_sides("wout0", self._stage_gate)
                self._side = wops
            self._qsrc = qbuf[hp % 2]
            self._ksrc = kbuf[hp % 2]
            self._vsrc = vpar[hp % 2]
            em.wait("tensor", S["dve"], rot_mark)
            if hp == 0:
                self.dump_one("qt", self.qtp[:, :])
                self.dump_one("kv", self.kvp[:, :])
                self.dump_one("vp", self.vp.ap()[:, :])
            self._attn_loop(hp, causal=False)
            if hp + 1 < NP:
                rot_mark = S["dve"].n
            pe_proj_mark[(hp + 1) % 2] = S["pe"].n
        self._qsrc = None
        self._ksrc = None
        self._vsrc = None
        self.proj_full("wout0", self.outT, self.xn, bias=self.bias["bout0"],
                       prestream=self._wout0_pre)
        self._stage_gate2 = (S["act"].n, S["dve"].n, S["pool"].n)
        self.dump_full("z", self.xn)
        self.emit_ln(srcbf=self.chunks(self.xn), mode="add")

    def _attn_loop(self, hp, causal):
        em, S = self.em, self.S
        Sps = [self.psA.ap()[:, i * 1024:(i + 1) * 1024] for i in range(2)]
        AVps = self.psA.ap()[:, 2048:2560]
        Lps = self.psA.ap()[0:64, 2560:3072]
        qsrc = getattr(self, "_qsrc", None) or self.qtp
        ksrc = getattr(self, "_ksrc", None) or self.kvp
        side = getattr(self, "_side", None) or []
        self._side = None
        side_i = 0
        QB = 512
        DEPTH = 2
        self._ptv = [self.pt[0][:, 0:1024], self.pt[0][:, 1024:2048], self.pt[1][:, 0:1024]]
        prev_exp_end = getattr(self, "_attn_prev_exp_end", 0)
        prev_fin_dve = getattr(self, "_attn_prev_fin_dve", 0)
        prev_fin_act = getattr(self, "_attn_prev_fin_act", 0)
        for qb in range(0, 2048, QB):
            kc_max = 16 if not causal else (qb + QB) // 128
            exp_cnt = []
            em.wait("tensor", S["act"], prev_exp_end)
            fin_dve_gate = prev_fin_dve
            fin_act_gate = prev_fin_act
            for kc in range(kc_max):
                off = kc * 128 - qb if causal else -1
                lo = max(0, off)
                Sb = Sps[kc % 2]
                pt = self._ptv[kc % 3]
                if kc >= 2:
                    em.wait("tensor", S["act"], exp_cnt[kc - 2])
                for (hh, rlo, tp) in ((0, 0, (0, 0)), (1, 64, (64, 0))):
                    em.do("tensor", lambda e, hh=hh, rlo=rlo, tp=tp, lo=lo, kc=kc, qb=qb, Sb=Sb: e.matmul(
                        Sb[:, hh * 512 + lo: (hh + 1) * 512],
                        ksrc[rlo:rlo + 64, kc * 128:(kc + 1) * 128],
                        qsrc[rlo:rlo + 64, qb + lo: qb + 512],
                        start=True, stop=True, tile_position=tp, skip_group_check=True))
                pe_after_S = S["pe"].n
                if kc == 1:
                    em.wait("tensor", S["dve"], fin_dve_gate)
                    em.wait("tensor", S["act"], fin_act_gate)
                if kc >= 1:
                    self._avl512(hp, qb, kc - 1, kc_max, causal, AVps, Lps)
                em.wait("scalar", S["pe"], pe_after_S)
                em.do("scalar", lambda e, pt=pt, Sb=Sb: e.activation(
                    out=pt[:, 0:1024], in_=Sb, func=AF.Exp, bias=SHIFT, scale=SCALE8))
                exp_cnt.append(S["act"].n)
                if causal and 0 <= off < QB:
                    em.wait("vector", S["act"], exp_cnt[kc])
                    for hh in range(2):
                        em.do("vector", lambda e, hh=hh, off=off, pt=pt: e.tensor_tensor(
                            out=pt[:, hh * 512 + off: hh * 512 + off + 128],
                            in0=pt[:, hh * 512 + off: hh * 512 + off + 128],
                            in1=self.tril[:], op=ALU.mult))
                for _ in range(3):
                    if side_i < len(side):
                        side[side_i]()
                        side_i += 1
            self._avl512(hp, qb, kc_max - 1, kc_max, causal, AVps, Lps, last=True)
            prev_exp_end = exp_cnt[-1]
            rlB = self.psA.ap()[:, 2560:3072]
            rf = self.rowb.ap().bitcast(F32)
            em.wait("scalar", S["pe"])
            em.do("scalar", lambda e, rf=rf: e.activation(out=rf[0:33, 0:512], in_=Lps[0:33, 0:512], func=AF.Ln))
            em.do("scalar", lambda e, rf=rf: e.activation(out=rf[0:33, 0:512], in_=rf[0:33, 0:512], func=AF.Exp, scale=f32(-1.0)))
            em.wait("vector", S["act"])
            em.do("vector", lambda e, rf=rf: e.tensor_copy(out=self.rowb2[0:1, 0:512], in_=rf[0:1, 0:512]))
            em.do("vector", lambda e, rf=rf: e.tensor_copy(out=self.rowb2[32:33, 0:512], in_=rf[32:33, 0:512]))
            em.wait("tensor", S["dve"])
            em.wait("tensor", S["act"])
            em.do("tensor", lambda e, rlB=rlB: e.matmul(rlB, self.e33[:], self.rowb2[0:33, 0:512],
                  start=True, stop=True, skip_group_check=True))
            em.wait("vector", S["pe"])
            em.do("vector", lambda e, rlB=rlB: e.tensor_copy(out=self.lncast[:, 0:512], in_=rlB))
            em.do("vector", lambda e, hp=hp, qb=qb: e.tensor_tensor(
                out=self.outT.ap()[:, hp * T + qb: hp * T + qb + 512],
                in0=AVps, in1=self.lncast[:, 0:512], op=ALU.mult))
            prev_fin_dve = S["dve"].n
            prev_fin_act = S["act"].n
        while side_i < len(side):
            side[side_i]()
            side_i += 1
        self._attn_prev_exp_end = prev_exp_end
        self._attn_prev_fin_dve = prev_fin_dve
        self._attn_prev_fin_act = prev_fin_act

    def _avl512(self, hp, qb, kc, kc_max, causal, AVps, Lps, last=False):
        em, S = self.em, self.S
        off = kc * 128 - qb if causal else -1
        lo = max(0, off)
        if kc == 0 and getattr(self, "_vp_gate", 0):
            em.wait("tensor", S["syd"], self._vp_gate)
        if causal and 0 <= off < 512:
            em.wait("tensor", S["dve"])
        else:
            em.wait("tensor", S["act"])
        pt = self._ptv[kc % 3]
        vsrc = getattr(self, "_vsrc", None)
        if vsrc is None:
            vsrc = self.vp.ap()[:, :]
        for (hh, tpav) in ((0, (0, 0)), (1, (0, 64))):
            em.do("tensor", lambda e, hh=hh, tpav=tpav, lo=lo, kc=kc, pt=pt, vsrc=vsrc: e.matmul(
                AVps[hh * 64:(hh + 1) * 64, lo:512],
                vsrc[:, kc * 128 + hh * 64: kc * 128 + hh * 64 + 64],
                pt[:, hh * 512 + lo: (hh + 1) * 512],
                start=(kc == 0), stop=(kc == kc_max - 1),
                tile_position=tpav, skip_group_check=True))
        for (hh, tpl) in ((0, (0, 0)), (1, (0, 32))):
            em.do("tensor", lambda e, hh=hh, tpl=tpl, lo=lo, kc=kc, pt=pt: e.matmul(
                Lps[hh * 32:hh * 32 + 32, lo:512],
                self.onespad[:], pt[:, hh * 512 + lo: (hh + 1) * 512],
                start=(kc == 0), stop=(kc == kc_max - 1),
                tile_position=tpl, skip_group_check=True))

    def emit_store(self):
        em, S = self.em, self.S
        marks = getattr(self, "_add_marks", {})
        em.wait("scalar", S["pool"])
        for c in range(CH):
            em.wait("scalar", S["dve"], marks.get(c, S["dve"].n))
            if c >= 1:
                em.wait("scalar", S["syd"])
            for hf in range(2):
                em.do("scalar", lambda e, c=c, hf=hf: e.copy(
                    out=self.stage[hf][:],
                    in_=self.xb.ap()[:, c * T + hf * 1024: c * T + (hf + 1) * 1024]))
            em.wait("sync", S["act"])
            for hf in range(2):
                em.do("sync", lambda e, c=c, hf=hf: e.dma_start(
                    out=self.x_out[c * 128:(c + 1) * 128, hf * 1024:(hf + 1) * 1024],
                    in_=self.stage[hf][:]), inc=S["sy"], inc_n=16)
            self.dsig()
        em.wait("sync", S["sy"])

    def emit_l0f(self):
        em, S = self.em, self.S
        P = self.pairs.ap()
        atbf = P[:, 0:1024]
        btbf = P[:, 1024:2048]
        rtbf_cols = 4608
        rtbf = P[:, rtbf_cols:rtbf_cols + 64]
        lgbf = P[:, 2048:4096]
        gtbf = P[:, 6144:8192]
        hbf_cols = 0
        self.stream_weight("aT", atbf, n_chunks=CH, cols=128,
                           sync_gate=self._stage_gate2, stage_off=0, cast_eng="scalar")
        self.stream_weight("bT", btbf, n_chunks=1, cols=1024,
                           sync_gate=self._stage_gate2, stage_off=1, cast_eng="scalar")
        self.stream_weight("rT", rtbf, n_chunks=CH, cols=8, cast_eng="scalar",
                           sync_gate=(S["act"].n, S["dve"].n, S["pool"].n))
        self._l0f_gate = (S["act"].n, S["dve"].n, S["pool"].n)
        em.wait("tensor", S["act"])
        self.emit_ln(use_x32_cast=True, dst=self.chunks(self.xn), mode="norm",
                     src_marks=getattr(self, "_add_marks", None),
                     sq_eng="scalar", sq_scratch=self.chunks(self.xn))
        xnm = dict(self._apply_marks)
        lg_ps = self.psA.ap()[0:8, 2048:4096]
        for kc in range(CH):
            em.wait("tensor", S["dve"], xnm[kc])
            for j in range(4):
                em.do("tensor", lambda e, kc=kc, j=j: e.matmul(
                    lg_ps[:, j * 512:(j + 1) * 512], rtbf[:, kc * 8:(kc + 1) * 8],
                    self.xn.ap()[:, kc * T + j * 512:kc * T + (j + 1) * 512],
                    start=(kc == 0), stop=(kc == CH - 1), skip_group_check=True),
                    inc=S["pe"] if (kc == CH - 1 and j == 3) else None)
        em.wait("scalar", S["pe"])
        em.do("scalar", lambda e: e.copy(out=lgbf[0:8, :], in_=lg_ps))
        h_ps = self.psA.ap()[:, 0:2048]
        for kc in range(CH):
            for j in range(4):
                em.do("tensor", lambda e, kc=kc, j=j: e.matmul(
                    h_ps[:, j * 512:(j + 1) * 512], atbf[:, kc * 128:(kc + 1) * 128],
                    self.xn.ap()[:, kc * T + j * 512:kc * T + (j + 1) * 512],
                    start=(kc == 0), stop=(kc == CH - 1), skip_group_check=True),
                    inc=S["pe"] if (kc == CH - 1 and j == 3) else None)
        em.wait("vector", S["pe"])
        hbf = self.pt[0]
        em.do("vector", lambda e, hbf=hbf: e.tensor_copy(out=hbf[:, :], in_=h_ps))
        hbf_mark = S["dve"].n
        rbf32 = self.rowb.ap().bitcast(F32)
        gl = rbf32[:, 0:128]
        em.wait("tensor", S["act"])
        for tt in range(16):
            pso = self.psA.ap()[:, 2048 + (tt % 2) * 512: 2048 + (tt % 2) * 512 + 4].bitcast(BF16)
            if tt >= 2:
                em.wait("tensor", S["dve"])
            em.do("tensor", lambda e, tt=tt, pso=pso: e.transpose(
                pso, lgbf[0:8, tt * 128:(tt + 1) * 128], self.ident[0:8, 0:8]),
                inc=S["pe"])
            em.wait("vector", S["pe"])
            em.do("vector", lambda e, tt=tt, pso=pso: e.tensor_copy(
                out=gl[:, tt * 8:(tt + 1) * 8], in_=pso))
        gl3 = gl.rearrange("p (a b) -> p a b", b=8)
        mx8 = rbf32[:, 128:256]
        for tt in range(16):
            em.do("vector", lambda e, tt=tt: e.max(out=mx8[:, tt * 8:(tt + 1) * 8],
                  in_=gl[:, tt * 8:(tt + 1) * 8]))
        mx83 = mx8.rearrange("p (a b) -> p a b", b=8)
        tl = rbf32[:, 256:384]
        tl3 = tl.rearrange("p (a b) -> p a b", b=8)
        em.do("vector", lambda e: e.tensor_tensor(out=tl3, in0=gl3,
              in1=mx83[:, :, 0:1].broadcast_to([128, 16, 8]), op=ALU.subtract))
        g = rbf32[:, 384:512]
        em.wait("scalar", S["dve"])
        em.do("scalar", lambda e: e.activation(out=g, in_=tl, func=AF.Exp))
        g3 = g.rearrange("p (a b) -> p a b", b=8)
        s1 = rbf32[:, 512:528]
        em.wait("vector", S["act"])
        em.do("vector", lambda e: e.tensor_reduce(out=s1, in_=g3, axis=mybir.AxisListType.X, op=ALU.add))
        keep = rbf32[:, 256:384]
        keep3 = tl3
        em.do("vector", lambda e: e.tensor_tensor(out=keep3, in0=gl3,
              in1=mx83[:, :, 1:2].broadcast_to([128, 16, 8]), op=ALU.is_ge))
        gk3 = g3
        em.do("vector", lambda e: e.tensor_tensor(out=gk3, in0=g3, in1=keep3, op=ALU.mult))
        sm = rbf32[:, 528:544]
        em.do("vector", lambda e: e.tensor_reduce(out=sm, in_=gk3, axis=mybir.AxisListType.X, op=ALU.add))
        r1 = rbf32[:, 544:560]
        em.do("vector", lambda e: e.reciprocal(out=r1, in_=s1))
        t2 = rbf32[:, 560:576]
        em.do("vector", lambda e: e.tensor_tensor(out=t2, in0=sm, in1=r1, op=ALU.mult))
        em.do("vector", lambda e: e.tensor_scalar_add(out=t2, in0=t2, scalar1=1e-6))
        rt = rbf32[:, 576:592]
        em.do("vector", lambda e: e.reciprocal(out=rt, in_=t2))
        sc = rbf32[:, 592:608]
        em.do("vector", lambda e: e.tensor_tensor(out=sc, in0=r1, in1=rt, op=ALU.mult))
        sc3 = sc.rearrange("p (a b) -> p a b", b=1)
        em.do("vector", lambda e: e.tensor_tensor(
            out=self.gate_bf.ap().rearrange("p (a b) -> p a b", b=8), in0=gk3,
            in1=sc3.broadcast_to([128, 16, 8]), op=ALU.mult), inc=S["dve"])
        self.dump_one("gate", self.gate_bf.ap()[:, :])
        em.wait("tensor", S["dve"])
        for tt in range(16):
            pso = self.psA.ap()[:, 2048 + (tt % 2) * 512: 2048 + (tt % 2) * 512 + 64].bitcast(BF16)
            if tt >= 2:
                em.wait("tensor", S["dve"])
            em.do("tensor", lambda e, tt=tt, pso=pso: e.transpose(
                pso[0:8, :], self.gate_bf.ap()[:, tt * 8:(tt + 1) * 8], self.ident[:]),
                inc=S["pe"])
            em.wait("vector", S["pe"])
            em.do("vector", lambda e, tt=tt, pso=pso: e.tensor_copy(
                out=gtbf[0:8, tt * 128:(tt + 1) * 128], in_=pso[0:8, :]), inc=S["dve"])
        em.wait("tensor", S["dve"])
        gB = self.psA.ap()[:, 2048:4096]
        for j in range(4):
            em.do("tensor", lambda e, j=j: e.matmul(gB[:, j * 512:(j + 1) * 512],
                  self.selER[:], gtbf[0:8, j * 512:(j + 1) * 512], start=True, stop=True,
                  skip_group_check=True), inc=S["pe"] if j == 3 else None)
        em.wait("vector", S["pe"])
        ghbf = self.pt[1]
        em.do("vector", lambda e, ghbf=ghbf: e.tensor_tensor(out=ghbf[:, :], in0=self.pt[0][:, :],
              in1=gB, op=ALU.mult))
        gh_mark = S["dve"].n
        self.dump_one("h0", self.pt[0][:, :])
        self.dump_one("gh", ghbf[:, :])
        self.stream_weight("w0T", self.warena.ap(), n_chunks=CH, cols=1024,
                           cast_eng="scalar", sync_gate=self._l0f_gate, stage_off=1)
        em.wait("tensor", S["act"])
        for oc in range(CH):
            ps = self.psA.ap()[:, (oc % 2) * 2048:(oc % 2) * 2048 + 2048]
            if oc == 0:
                em.wait("tensor", S["dve"], hbf_mark)
            elif oc == 1:
                em.wait("tensor", S["dve"], gh_mark)
            else:
                em.wait("tensor", S["dve"])
            for kc in range(CH):
                for j in range(4):
                    em.do("tensor", lambda e, oc=oc, kc=kc, j=j, ps=ps: e.matmul(
                        ps[:, j * 512:(j + 1) * 512],
                        self.warena.ap()[:, kc * 1024 + oc * 128: kc * 1024 + (oc + 1) * 128],
                        self.xn.ap()[:, kc * T + j * 512: kc * T + (j + 1) * 512],
                        start=(kc == 0), stop=False, skip_group_check=True))
            if oc == 0:
                em.wait("tensor", S["dve"], gh_mark)
            for j in range(4):
                em.do("tensor", lambda e, oc=oc, j=j, ps=ps, ghbf=ghbf: e.matmul(
                    ps[:, j * 512:(j + 1) * 512],
                    btbf[:, oc * 128:(oc + 1) * 128],
                    ghbf[:, j * 512:(j + 1) * 512],
                    start=False, stop=True, skip_group_check=True),
                    inc=S["pe"] if j == 3 else None)
            em.wait("vector", S["pe"])
            em.do("vector", lambda e, oc=oc, ps=ps: e.tensor_scalar_add(
                out=self.outT.ap()[:, oc * T:(oc + 1) * T], in0=ps,
                scalar1=self.bias["b0"][:, oc:oc + 1]), inc=S["dve"])
        em.wait("tensor", S["dve"])
        if "l1a" in self.blocks:
            self.l1a_prefetch0()
        self.dump_full("y", self.outT)
        self.emit_ln(srcbf=self.chunks(self.outT), mode="add")

    def l1a_scratch(self):
        rotf32 = self.rotc.ap().bitcast(F32)
        wf32 = self.warena.ap().bitcast(F32)
        return (("wq1", rotf32[:, 0:1024], None),
                ("wk1", rotf32[:, 1024:2048], None),
                ("wv1", wf32[:, 1536:2048], wf32[:, 3584:4096]))

    def l1a_prefetch0(self):
        em, S = self.em, self.S
        em.wait("sync", S["pe"])
        em.wait("sync", S["dve"])
        em.wait("sync", S["pool"])
        em.wait("sync", S["act"])
        for (wname, scrA, scrB) in self.l1a_scratch():
            src3 = self.ins[wname].rearrange("(c p) o -> p c o", p=128)[:, :, 0:128]
            if scrB is None:
                em.do("sync", lambda e, scr=scrA, src3=src3: e.dma_start(
                    out=scr.rearrange("p (c o) -> p c o", c=8), in_=src3), inc=S["sy"], inc_n=16)
                self.dsig()
            else:
                for (scr, k0) in ((scrA, 0), (scrB, 4)):
                    em.do("sync", lambda e, scr=scr, src3=src3, k0=k0: e.dma_start(
                        out=scr.rearrange("p (c o) -> p c o", c=4),
                        in_=src3[:, k0:k0 + 4, :]), inc=S["sy"], inc_n=16)
                    self.dsig()
        self._l1a_prefetched = True

    def emit_l1a(self):
        em, S = self.em, self.S
        self._l1a_gate = (S["act"].n, S["dve"].n, S["pool"].n)
        qbuf = [self.qtp, self.scr16.ap()[:, T:2 * T]]
        kbuf = [self.kvp, self.scr16.ap()[:, 2 * T:3 * T]]
        vbuf = [self.vp.ap()[:, 0:T], self.vp.ap()[:, T:2 * T]]
        self.vp2 = None
        pe_proj_mark = [0, 0]

        l1a_scr = self.l1a_scratch()

        def l1a_weights_dma(hp, dve_gate):
            def emit():
                em.wait("sync", S["dve"], dve_gate)
                for (wname, scrA, scrB) in l1a_scr:
                    src3 = self.ins[wname].rearrange("(c p) o -> p c o", p=128)[:, :, hp * 128:(hp + 1) * 128]
                    if scrB is None:
                        em.do("sync", lambda e, scr=scrA, src3=src3: e.dma_start(
                            out=scr.rearrange("p (c o) -> p c o", c=8), in_=src3),
                            inc=S["sy"], inc_n=16)
                        self.dsig()
                    else:
                        for (scr, k0) in ((scrA, 0), (scrB, 4)):
                            em.do("sync", lambda e, scr=scr, src3=src3, k0=k0: e.dma_start(
                                out=scr.rearrange("p (c o) -> p c o", c=4),
                                in_=src3[:, k0:k0 + 4, :]), inc=S["sy"], inc_n=16)
                            self.dsig()
            return emit

        def l1a_weights_cast(hp, pe_guard):
            slot = (hp % 2) * 4096
            wq_p = self.warena.ap()[:, slot:slot + 1024]
            wk_p = self.warena.ap()[:, slot + 1024:slot + 2048]
            wv_p = self.warena.ap()[:, slot + 2048:slot + 3072]
            marks = {}
            em.wait("vector", S["syd"])
            em.wait("vector", S["pe"], pe_guard)
            for (wname, scrA, scrB), dstw in zip(l1a_scr, (wq_p, wk_p, wv_p)):
                if scrB is None:
                    em.do("vector", lambda e, scr=scrA, d=dstw: e.tensor_copy(out=d, in_=scr))
                else:
                    em.do("vector", lambda e, scr=scrA, d=dstw: e.tensor_copy(out=d[:, 0:512], in_=scr))
                    em.do("vector", lambda e, scr=scrB, d=dstw: e.tensor_copy(out=d[:, 512:1024], in_=scr))
                marks[wname] = (S["dve"].n, S["pool"].n)
            return wq_p, wk_p, wv_p, marks

        def proj_inline(hp, wsrc, dstp, bn, marks=None):
            em.wait("tensor", S["pool"])
            if marks is None:
                em.wait("tensor", S["dve"])
            else:
                em.wait("tensor", S["dve"], getattr(self, "_ps_evac_mark", 0))
            em.wait("tensor", S["act"])
            ps = self.psA.ap()[:, 0:2048]
            for kc in range(CH):
                if marks is not None:
                    em.wait("tensor", S["dve"], marks[kc])
                for j in range(4):
                    em.do("tensor", lambda e, kc=kc, j=j, w=wsrc, ps=ps: e.matmul(
                        ps[:, j * 512:(j + 1) * 512], w[:, kc * 128:(kc + 1) * 128],
                        self.xn.ap()[:, kc * T + j * 512: kc * T + (j + 1) * 512],
                        start=(kc == 0), stop=(kc == CH - 1), skip_group_check=True))
            em.wait("vector", S["pe"])
            em.do("vector", lambda e, d=dstp, ps=ps, bn=bn, hp=hp: e.tensor_scalar_add(
                out=d[:, :], in0=ps, scalar1=self.bias[bn][:, hp:hp + 1]))
            self._ps_evac_mark = S["dve"].n

        if not hasattr(self, "_side_bank_mark"):
            self._side_bank_mark = {0: 0, 1: 0}

        def proj_sides(hp, wsrc, dstp, bn, marks, wname):
            ops = []
            for j in range(4):
                pb = self.psA.ap()[:, 3072 + (j % 2) * 512: 3072 + (j % 2) * 512 + 512]
                def mk_mm(kc0, j=j, pb=pb, first=(j == 0)):
                    def emit():
                        if first and kc0 == 0:
                            dn, pn = marks[wname]
                            em.wait("tensor", S["dve"], dn)
                            em.wait("tensor", S["pool"], pn)
                        if kc0 == 0:
                            em.wait("tensor", S["dve"], self._side_bank_mark[j % 2])
                        for kc in (kc0, kc0 + 1):
                            em.do("tensor", lambda e, kc=kc, j=j, pb=pb: e.matmul(
                                pb, wsrc[:, kc * 128:(kc + 1) * 128],
                                self.xn.ap()[:, kc * T + j * 512: kc * T + (j + 1) * 512],
                                start=(kc == 0), stop=(kc == CH - 1), skip_group_check=True))
                    return emit
                for kc0 in (0, 2, 4, 6):
                    ops.append(mk_mm(kc0))
                def mk_evac(j=j, pb=pb):
                    def emit():
                        em.wait("vector", S["pe"])
                        em.do("vector", lambda e, j=j, pb=pb: e.tensor_scalar_add(
                            out=dstp[:, j * 512:(j + 1) * 512], in0=pb,
                            scalar1=self.bias[bn][:, hp:hp + 1]))
                        self._side_bank_mark[j % 2] = S["dve"].n
                    return emit
                ops.append(mk_evac())
            return ops

        def vprod_sides(hp, wv_p, marks):
            ops = []
            vpd = self.vp2
            for tt in range(16):
                pb = self.psA.ap()[:, 3072 + (tt % 2) * 512: 3072 + (tt % 2) * 512 + 128]
                def mk_mm(kc0, tt=tt, pb=pb):
                    def emit():
                        if tt == 0 and kc0 == 0:
                            dn, pn = marks["wv1"]
                            em.wait("tensor", S["dve"], dn)
                            em.wait("tensor", S["pool"], pn)
                        if kc0 == 0:
                            em.wait("tensor", S["dve"], self._side_bank_mark[tt % 2])
                        for kc in range(kc0, min(kc0 + 3, CH)):
                            em.do("tensor", lambda e, tt=tt, kc=kc, pb=pb: e.matmul(
                                pb, self.xn.ap()[:, kc * T + tt * 128: kc * T + (tt + 1) * 128],
                                wv_p[:, kc * 128:(kc + 1) * 128],
                                start=(kc == 0), stop=False, skip_group_check=True))
                        if kc0 + 3 >= CH:
                            em.do("tensor", lambda e, tt=tt, pb=pb, hp=hp: e.matmul(
                                pb, self.ones1r[:], self.bvrow[0:1, hp * 128:(hp + 1) * 128],
                                start=False, stop=True, skip_group_check=True))
                    return emit
                for kc0 in (0, 3, 6):
                    ops.append(mk_mm(kc0))
                def mk_evac(tt=tt, pb=pb, vpd=vpd):
                    def emit():
                        em.wait("vector", S["pe"])
                        em.do("vector", lambda e, tt=tt, pb=pb, vpd=vpd: e.tensor_copy(
                            out=vpd[:, tt * 128:(tt + 1) * 128], in_=pb))
                        self._side_bank_mark[tt % 2] = S["dve"].n
                    return emit
                ops.append(mk_evac())
            return ops

        def vprod_inline(hp, wv_p, marks=None):
            if marks is None:
                em.wait("tensor", S["dve"])
            em.wait("tensor", S["act"])
            for tt in range(16):
                psv = self.psA.ap()[:, 2048 + (tt % 2) * 512: 2048 + (tt % 2) * 512 + 128]
                if tt >= 2:
                    em.wait("tensor", S["dve"])
                for kc in range(CH):
                    if marks is not None and tt == 0:
                        em.wait("tensor", S["dve"], marks[kc])
                    em.do("tensor", lambda e, tt=tt, kc=kc, psv=psv, wv_p=wv_p: e.matmul(
                        psv, self.xn.ap()[:, kc * T + tt * 128: kc * T + (tt + 1) * 128],
                        wv_p[:, kc * 128:(kc + 1) * 128],
                        start=(kc == 0), stop=False, skip_group_check=True))
                em.do("tensor", lambda e, tt=tt, psv=psv, hp=hp: e.matmul(
                    psv, self.ones1r[:], self.bvrow[0:1, hp * 128:(hp + 1) * 128],
                    start=False, stop=True, skip_group_check=True))
                em.wait("vector", S["pe"])
                em.do("vector", lambda e, tt=tt, psv=psv: e.tensor_copy(
                    out=self.vp.ap()[:, tt * 128:(tt + 1) * 128], in_=psv))
            em.wait("tensor", S["dve"])

        if not getattr(self, "_l1a_prefetched", False):
            l1a_weights_dma(0, 0)()
        wq0, wk0, wv0, marks0 = l1a_weights_cast(0, 0)
        l1a_weights_dma(1, S["dve"].n)()
        self.emit_ln(use_x32_cast=True, dst=self.chunks(self.xn), mode="norm",
                     src_marks=getattr(self, "_add_marks", None),
                     sq_eng="scalar", sq_scratch=self.chunks(self.xn))
        xnm1 = dict(self._apply_marks)
        proj_inline(0, wq0, qbuf[0], "bq1", marks=xnm1)
        proj_inline(0, wk0, kbuf[0], "bk1", marks=xnm1)
        pe_proj_mark[0] = S["pe"].n
        vprod_inline(0, wv0, marks=xnm1)
        self._vp_gate = 0
        for hp in range(NP):
            if hp + 1 < NP:
                wqn, wkn, wvn, marksn = l1a_weights_cast(hp + 1, pe_proj_mark[(hp + 1) % 2])
                self.vp2 = vbuf[(hp + 1) % 2]
                sides = (proj_sides(hp + 1, wqn, qbuf[(hp + 1) % 2], "bq1", marksn, "wq1")
                         + proj_sides(hp + 1, wkn, kbuf[(hp + 1) % 2], "bk1", marksn, "wk1")
                         + vprod_sides(hp + 1, wvn, marksn))
                if hp + 2 < NP:
                    sides = sides + [l1a_weights_dma(hp + 2, S["dve"].n)]
                self._side = sides
            else:
                wops, self._wo1_pre = self.prestream_sides("wo1", self._l1a_gate)
                self._side = wops
            self._qsrc = qbuf[hp % 2]
            self._ksrc = kbuf[hp % 2]
            self._vsrc = vbuf[hp % 2]
            if hp == 0:
                self.dump_one("qt", self.qtp[:, :])
                self.dump_one("kv", self.kvp[:, :])
                self.dump_one("vp", self.vp.ap()[:, :])
            self._attn_loop(hp, causal=True)
            pe_proj_mark[(hp + 1) % 2] = S["pe"].n
        self._vsrc = None
        self._qsrc = None
        self._ksrc = None
        self.proj_full("wo1", self.outT, self.xn, bias=self.bias["bo1"],
                       prestream=self._wo1_pre)
        self.dump_full("z", self.xn)
        self.emit_ln(srcbf=self.chunks(self.xn), mode="add")

    def emit_l1f(self):
        em, S = self.em, self.S
        self._l1f_gate = (S["act"].n, S["dve"].n, S["pool"].n)
        self.emit_ln(use_x32_cast=True, dst=self.chunks(self.xn), mode="norm",
                     src_marks=getattr(self, "_add_marks", None),
                     sq_eng="scalar", sq_scratch=self.chunks(self.xn))
        tm = dict(self._apply_marks)
        self.dump_full("t", self.xn)
        em.wait("sync", S["dve"])
        em.wait("sync", S["pool"])
        for c in range(CH):
            em.do("sync", lambda e, c=c: e.dma_start(
                out=self.x_spill[c * 128:(c + 1) * 128, :],
                in_=self.xb.ap()[:, c * T:(c + 1) * T]), inc=S["sy"], inc_n=16)
        self.dsig()
        spill_sig = S["syd"].n
        self.emit_ln(srcbf=self.chunks(self.xn), mode="gamma",
                     gamma=self.bias["lnsc"], beta=self.bias["lnbi"], half=0,
                     dst=lambda c: self.pairs.ap()[:, c * 1024:(c + 1) * 1024],
                     src_marks=tm, sq_eng="scalar",
                     sq_scratch=self.chunks(self.outT))
        um = dict(self._apply_marks)
        self.stream_weight("awT", self.warena.ap(), n_chunks=CH, cols=1024,
                           cast_eng="scalar", sync_gate=self._l1f_gate)
        for hf in range(2):
            if hf == 1:
                self.ln_apply(srcbf=self.chunks(self.xn), mode="gamma",
                              gamma=self.bias["lnsc"], beta=self.bias["lnbi"], half=1,
                              dst=lambda c: self.pairs.ap()[:, c * 1024:(c + 1) * 1024])
                um = dict(self._apply_marks)
            em.wait("tensor", S["act"])
            for oc in range(CH):
                ps = self.psA.ap()[:, (oc % 4) * 1024:(oc % 4) * 1024 + 1024]
                if oc >= 4:
                    em.wait("tensor", S["dve"])
                for kc in range(CH):
                    if oc == 0:
                        em.wait("tensor", S["dve"], um[kc])
                    for j in range(2):
                        em.do("tensor", lambda e, oc=oc, kc=kc, j=j, ps=ps: e.matmul(
                            ps[:, j * 512:(j + 1) * 512],
                            self.warena.ap()[:, kc * 1024 + oc * 128: kc * 1024 + (oc + 1) * 128],
                            self.pairs.ap()[:, kc * 1024 + j * 512: kc * 1024 + (j + 1) * 512],
                            start=(kc == 0), stop=(kc == CH - 1), skip_group_check=True),
                            inc=S["pe"] if (kc == CH - 1 and j == 1) else None)
                em.wait("vector", S["pe"])
                em.do("vector", lambda e, oc=oc, ps=ps, hf=hf: e.scalar_tensor_tensor(
                    out=self.outT.ap()[:, oc * T + hf * 1024: oc * T + hf * 1024 + 1024],
                    in0=ps, scalar=self.bias["ab1"][:, oc:oc + 1],
                    in1=self.xn.ap()[:, oc * T + hf * 1024: oc * T + hf * 1024 + 1024],
                    op0=ALU.add, op1=ALU.add), inc=S["dve"])
            em.wait("tensor", S["dve"])
        if "xa" in self.dbg_outs:
            self.dump_full("xa", self.outT)
        sx = "ab"
        def hchunk(hh):
            if hh < 16:
                return self.xb.ap()[:, hh * 1024:(hh + 1) * 1024]
            return self.xn.ap()[:, (hh - 16) * 1024:(hh - 15) * 1024]
        for hf in range(2):
            w1n, w2n = "w1" + sx[hf], "w2" + sx[hf]
            b1n, b2n = "b1" + sx[hf], "b2" + sx[hf]
            w1_pe_marks = {}
            w1_cast_marks = {}
            def w1_stream(hs):
                wsl = self.warena.ap()[:, (hs % 2) * 4096:(hs % 2) * 4096 + 4096]
                self.stream_weight(w1n, wsl, n_chunks=CH, cols=512, col0=hs * 512,
                                   pe_guard=w1_pe_marks.get(hs - 2, None))
                w1_cast_marks[hs] = (S["dve"].n, S["pool"].n)
                return wsl
            wsl_next = w1_stream(0)
            for hs in range(CH):
                wsl = wsl_next
                dn, pn = w1_cast_marks[hs]
                em.wait("tensor", S["dve"], dn)
                em.wait("tensor", S["pool"], pn)
                if hf == 0 and hs == 0:
                    em.wait("scalar", S["syd"], spill_sig)
                if hs + 1 < CH:
                    wsl_next = w1_stream(hs + 1)
                w1_pe_marks[hs] = None
                for hc in range(4):
                    hh = hs * 4 + hc
                    ps = self.psA.ap()[:, (hh % 4) * 1024:(hh % 4) * 1024 + 1024]
                    if hh >= 4:
                        em.wait("tensor", S["act"])
                    for kc in range(CH):
                        for j in range(2):
                            em.do("tensor", lambda e, hc=hc, kc=kc, j=j, ps=ps, hs=hs, wsl=wsl, hf=hf: e.matmul(
                                ps[:, j * 512:(j + 1) * 512],
                                wsl[:, kc * 512 + hc * 128: kc * 512 + (hc + 1) * 128],
                                self.outT.ap()[:, kc * T + hf * 1024 + j * 512: kc * T + hf * 1024 + (j + 1) * 512],
                                start=(kc == 0), stop=(kc == CH - 1), skip_group_check=True),
                                inc=S["pe"] if (kc == CH - 1 and j == 1) else None)
                    em.wait("scalar", S["pe"])
                    em.do("scalar", lambda e, hh=hh, ps=ps, b1n=b1n: e.activation(
                        out=hchunk(hh), in_=ps, func=AF.Gelu,
                        bias=self.bias[b1n][:, hh:hh + 1]))
                w1_pe_marks[hs] = S["pe"].n
            if hf == 0 and "h0" in self.dbg_outs:
                self.dump_one("h0", hchunk(0))
            em.do("scalar", lambda e: e.nop(), inc=S["act"])
            em.wait("tensor", S["act"])
            for ocg in range(2):
                self._w2_entry_pe = S["pe"].n
                em.wait("sync", S["pool"])
                em.wait("sync", S["dve"])
                cast_marks = []
                pe_marks = []
                def w2dma(hp2):
                    st = self.stage[hp2 % 2]
                    if hp2 >= 2:
                        sem, cnt = cast_marks[hp2 - 2]
                        em.wait("sync", sem, cnt)
                    srcw = self.ins[w2n].rearrange("(c p) o -> p c o", p=128)[
                        :, hp2 * 2:hp2 * 2 + 2, ocg * 512:(ocg + 1) * 512]
                    dsem = S["sy"] if hp2 % 2 == 0 else S["sy2"]
                    em.do("sync", lambda e, st=st, srcw=srcw: e.dma_start(
                        out=st[:].rearrange("p (c o) -> p c o", c=2), in_=srcw),
                        inc=dsem, inc_n=16)
                    return dsem
                dsems = [w2dma(0)]
                for hp2 in range(16):
                    if hp2 + 1 < 16:
                        dsems.append(w2dma(hp2 + 1))
                    em.wait("sync", dsems[hp2])
                    em.do("sync", lambda e: e.nop(), inc=S["syd"])
                    ceng = "vector"
                    em.wait(ceng, S["syd"])
                    if hp2 >= 2:
                        em.wait(ceng, S["pe"], pe_marks[hp2 - 2])
                    else:
                        em.wait(ceng, S["pe"], self._w2_entry_pe)
                    st = self.stage[hp2 % 2]
                    w2sl = self.warena.ap()[:, (hp2 % 2) * 4096:(hp2 % 2) * 4096 + 1024]
                    em.do(ceng, lambda e, st=st, w2sl=w2sl: e.tensor_copy(out=w2sl, in_=st[:]))
                    cs = S["dve"] if ceng == "vector" else S["pool"]
                    cast_marks.append((cs, cs.n))
                    em.wait("tensor", cs)
                    for sub in range(2):
                        hk = hp2 * 2 + sub
                        for oc4 in range(4):
                            for tq in range(2):
                                bank = oc4 * 2 + tq
                                em.do("tensor", lambda e, sub=sub, oc4=oc4, tq=tq, bank=bank, hk=hk, w2sl=w2sl, hf=hf: e.matmul(
                                    self.psA.ap()[:, bank * 512:(bank + 1) * 512],
                                    w2sl[:, sub * 512 + oc4 * 128: sub * 512 + (oc4 + 1) * 128],
                                    hchunk(hk)[:, tq * 512:(tq + 1) * 512],
                                    start=(hk == 0), stop=(hk == 31), skip_group_check=True))
                    pe_marks.append(S["pe"].n)
                    self._nw_gate = (S["act"].n, S["dve"].n, S["pool"].n)
                em.wait("vector", S["pe"])
                for oc4 in range(4):
                    oc = ocg * 4 + oc4
                    for tq in range(2):
                        bank = oc4 * 2 + tq
                        em.do("vector", lambda e, oc=oc, tq=tq, bank=bank, b2n=b2n: e.tensor_scalar_add(
                            out=self.eo_chunk(oc)[:, tq * 512:(tq + 1) * 512],
                            in0=self.psA.ap()[:, bank * 512:(bank + 1) * 512],
                            scalar1=self.bias[b2n][:, oc:oc + 1]),
                            inc=S["dve"] if (oc4 == 3 and tq == 1) else None)
                em.wait("tensor", S["dve"])
            if hf == 0 and "eo" in self.dbg_outs:
                for c in range(CH):
                    self.dump_one("eo", self.eo_chunk(c))
            self.stream_weight("nwT", self.warena.ap(), n_chunks=CH, cols=1024,
                               sync_gate=self._nw_gate)
            self.wait_w("tensor")
            if hf == 1:
                em.wait("sync", S["pe"])
                em.wait("sync", S["act"])
                for c in range(CH):
                    em.do("sync", lambda e, c=c: e.dma_start(
                        out=self.xb.ap()[:, c * T:(c + 1) * T],
                        in_=self.x_spill[c * 128:(c + 1) * 128, :]), inc=S["sy"], inc_n=16)
                self.dsig()
            for oc in range(CH):
                ps = self.psA.ap()[:, (oc % 4) * 1024:(oc % 4) * 1024 + 1024]
                if oc >= 4:
                    em.wait("tensor", S["act"])
                for kc in range(CH):
                    for j in range(2):
                        em.do("tensor", lambda e, oc=oc, kc=kc, j=j, ps=ps: e.matmul(
                            ps[:, j * 512:(j + 1) * 512],
                            self.warena.ap()[:, kc * 1024 + oc * 128: kc * 1024 + (oc + 1) * 128],
                            self.eo_chunk(kc)[:, j * 512:(j + 1) * 512],
                            start=(kc == 0), stop=(kc == CH - 1), skip_group_check=True),
                            inc=S["pe"] if (kc == CH - 1 and j == 1) else None)
                em.wait("scalar", S["pe"])
                em.do("scalar", lambda e, oc=oc, ps=ps, hf=hf: e.activation(
                    out=self.outT.ap()[:, oc * T + hf * 1024: oc * T + hf * 1024 + 1024],
                    in_=ps, func=AF.Gelu, bias=self.bias["nb"][:, oc:oc + 1]))
            em.do("scalar", lambda e: e.nop(), inc=S["act"])
            em.wait("tensor", S["act"])
        em.wait("vector", S["syd"])
        self.dump_full("y", self.outT)
        self.emit_ln(srcbf=self.chunks(self.outT), mode="add")

    def build(self):
        self.emit_consts()
        if "l0a" in self.blocks:
            self.emit_l0a()
        if "l0f" in self.blocks:
            self.emit_l0f()
        if "l1a" in self.blocks:
            self.emit_l1a()
        if "l1f" in self.blocks:
            self.emit_l1f()
        self.emit_store()
        with self.nc.Block() as block:
            self.em.replay(block)
        self._es.close()
        return self.nc


def rotary_tables():
    inv = 1.0 / 10000.0 ** (np.arange(0, DH, 2, dtype=np.float64) / DH)
    freqs = np.outer(np.arange(T, dtype=np.float64), inv)
    cos = np.cos(freqs).astype(np.float32)
    sin = np.sin(freqs).astype(np.float32)
    cosF = np.ascontiguousarray(np.tile(cos.T, (4, 1))).astype(bfl)
    sinF = np.ascontiguousarray(np.tile(sin.T, (4, 1))).astype(bfl)
    return cosF, sinF

def perm_matrix():
    P = np.zeros((128, 128), dtype=np.float32)
    for blk in range(2):
        b = blk * 64
        for i in range(32):
            P[b + i, b + 32 + i] = -1.0
            P[b + 32 + i, b + i] = 1.0
    return np.ascontiguousarray(P.T.astype(bfl))

def bias_table(b, n=CH):
    return np.ascontiguousarray(np.asarray(b, dtype=np.float32).reshape(n, 128).T)

def cm(a):
    return np.ascontiguousarray(np.asarray(a, dtype=np.float32))

def prep_weights(inputs, blocks):
    d = {}
    if "l0a" in blocks:
        cosF, sinF = rotary_tables()
        d.update(wq0=cm(inputs["L0_attn_Wq"]), wkv0=cm(inputs["L0_attn_Wkv"]),
                 wout0=cm(inputs["L0_attn_Wout"]), bout0=bias_table(inputs["L0_attn_bout"]),
                 cosf=cosF, sinf=sinF, permT=perm_matrix())
    if "l0f" in blocks:
        d.update(w0T=cm(np.asarray(inputs["L0_ffn_W"]).T),
                 rT=cm(np.asarray(inputs["L0_ffn_R"]).T),
                 aT=cm(np.asarray(inputs["L0_ffn_A"]).reshape(128, D).T),
                 bT=cm(np.asarray(inputs["L0_ffn_B"]).transpose(0, 2, 1).reshape(128, D)),
                 b0=bias_table(inputs["L0_ffn_b"]))
    if "l1a" in blocks:
        d.update(wq1=cm(inputs["L1_attn_Wq"]), wk1=cm(inputs["L1_attn_Wk"]),
                 wv1=cm(inputs["L1_attn_Wv"]), wo1=cm(inputs["L1_attn_Wo"]),
                 bq1=bias_table(inputs["L1_attn_bq"]), bk1=bias_table(inputs["L1_attn_bk"]),
                 bo1=bias_table(inputs["L1_attn_bo"]),
                 bv1row=cm(np.asarray(inputs["L1_attn_bv"]).reshape(1, D)))
    if "l1f" in blocks:
        d.update(lnsc=bias_table(inputs["L1_ffn_ln_scale"]), lnbi=bias_table(inputs["L1_ffn_ln_bias"]),
                 awT=cm(np.asarray(inputs["L1_ffn_attn_w"]).T), ab1=bias_table(inputs["L1_ffn_attn_b"]),
                 nwT=cm(np.asarray(inputs["L1_ffn_next_w"]).T), nb=bias_table(inputs["L1_ffn_next_b"]))
    return d

def prep_core_experts(inputs, core):
    eA = (2 * core) % 8
    eB = (2 * core + 1) % 8
    w1 = np.asarray(inputs["L1_ffn_w1"]); w2 = np.asarray(inputs["L1_ffn_w2"])
    b1 = np.asarray(inputs["L1_ffn_b1"]); b2 = np.asarray(inputs["L1_ffn_b2"])
    return dict(w1a=cm(w1[eA]), w1b=cm(w1[eB]), w2a=cm(w2[eA]), w2b=cm(w2[eB]),
                b1a=bias_table(b1[eA], n=32), b1b=bias_table(b1[eB], n=32),
                b2a=bias_table(b2[eA]), b2b=bias_table(b2[eB]))


_NC_CACHE = {}

def _build_nc():
    key = "full"
    if key not in _NC_CACHE:
        mk = MK(blocks=("l0a", "l0f", "l1a", "l1f"), debug=())
        _NC_CACHE[key] = mk.build()
    return _NC_CACHE[key]


def kernel(**inputs):
    nc = _build_nc()
    shared = prep_weights(inputs, ("l0a", "l0f", "l1a", "l1f"))
    x = np.asarray(inputs["L0_attn_x"], dtype=np.float32)
    in_maps = []
    for c in range(8):
        m = dict(shared)
        m.update(prep_core_experts(inputs, c))
        m["x_in"] = np.ascontiguousarray(x[c].T)
        in_maps.append(m)
    res = run_bass_kernel_spmd(nc, in_maps, core_ids=list(range(8)))
    out = np.stack([np.ascontiguousarray(r["x_out"].T) for r in res.results])
    return out.astype(np.float32)
